# version 03
# baseline (speedup 1.0000x reference)
import numpy as np
import ml_dtypes
import concourse.bass as bass
import concourse.mybir as mybir
from concourse.bass_utils import run_bass_kernel_spmd
from contextlib import ExitStack

F32 = mybir.dt.float32
BF16 = mybir.dt.bfloat16
AF = mybir.ActivationFunctionType
OP = mybir.AluOpType

HID = 2048
DS = 4096
S = 128
NH = 64
P = 64
KCONV = 4
CONV = DS + 2 * S
PROJ = 2 * DS + 2 * S + NH
T = 2048
TEXT = T + 128
NCH = T // 128
FFH = 8192
RSQ = 1.0 / np.sqrt(FFH)

ENG = ("sync", "tensor", "vector", "scalar", "gpsimd")


class Em:

    def __init__(self):
        self.ops = []
        self.counts = {e: 0 for e in ENG}
        self.last = None

    def emit(self, engine, fn, inc):
        prev = self.last
        self.counts[engine] += inc
        self.ops.append((engine, fn, inc, prev))
        self.last = (engine, self.counts[engine])


E = None


def dma_in(dst, src, slow=False):
    E.emit("sync", lambda e, d=dst, s=src, sl=slow: e.dma_start(
        out=d, in_=s, allow_slow_non_contiguous=sl) if sl else e.dma_start(out=d, in_=s), 16)


def dma_out(dst, src, slow=False):
    E.emit("gpsimd", lambda e, d=dst, s=src, sl=slow: e.dma_start(
        out=d, in_=s, allow_slow_non_contiguous=sl) if sl else e.dma_start(out=d, in_=s), 16)


def mm(out, lhsT, rhs, start=True, stop=True):
    E.emit("tensor", lambda e, o=out, l=lhsT, r=rhs, st=start, sp=stop:
           e.matmul(o, l, r, start=st, stop=sp), 1)


def petr(out, in_, idn):
    E.emit("tensor", lambda e, o=out, i=in_, d=idn: e.transpose(o, i, d), 1)


def act(out, in_, func, bias=0.0, scale=1.0):
    E.emit("scalar", lambda e, o=out, i=in_, f=func, b=bias, s=scale:
           e.activation(o, i, f, bias=b, scale=s), 1)


def tt(out, in0, in1, op):
    E.emit("vector", lambda e, o=out, a=in0, b=in1, p=op: e.tensor_tensor(o, a, b, p), 1)


def ts(out, in0, s1, op0, s2=None, op1=OP.bypass):
    E.emit("vector", lambda e, o=out, a=in0, x=s1, y=s2, p0=op0, p1=op1:
           e.tensor_scalar(o, a, x, y, p0, p1), 1)


def stt(out, in0, scalar, in1, op0, op1):
    E.emit("vector", lambda e, o=out, a=in0, s=scalar, b=in1, p0=op0, p1=op1:
           e.scalar_tensor_tensor(o, a, s, b, p0, p1), 1)


def ttscan(out, d0, d1, init, op0, op1):
    E.emit("vector", lambda e, o=out, a=d0, b=d1, i=init, p0=op0, p1=op1:
           e.tensor_tensor_scan(o, a, b, i, p0, p1), 1)


def vrecip(out, in_):
    E.emit("vector", lambda e, o=out, i=in_: e.reciprocal(o, i), 1)


def memset(ap, v):
    E.emit("vector", lambda e, a=ap, c=v: e.memset(a, c), 1)


def build_nc():
    global E
    E = Em()
    nc = bass.Bass("TRN2", target_bir_lowering=False, debug=False)

    x_fm = nc.dram_tensor("x_fm", [HID, TEXT], F32, kind="ExternalInput").ap()
    x_pfm = nc.dram_tensor("x_pfm", [HID, TEXT], F32, kind="ExternalInput").ap()
    maskc = nc.dram_tensor("maskc", [128, 1], F32, kind="ExternalInput").ap()
    Win = nc.dram_tensor("Win", [HID, PROJ], F32, kind="ExternalInput").ap()
    Wout = nc.dram_tensor("Wout", [DS, HID], F32, kind="ExternalInput").ap()
    W1T = nc.dram_tensor("W1T", [HID, FFH], F32, kind="ExternalInput").ap()
    W3T = nc.dram_tensor("W3T", [HID, FFH], F32, kind="ExternalInput").ap()
    W2q = nc.dram_tensor("W2q", [FFH, HID], F32, kind="ExternalInput").ap()
    convw = nc.dram_tensor("convw", [CONV, KCONV], F32, kind="ExternalInput").ap()
    convb = nc.dram_tensor("convb", [CONV, 1], F32, kind="ExternalInput").ap()
    mupc = nc.dram_tensor("mupc", [67 * 128, 1], F32, kind="ExternalInput").ap()
    normw = nc.dram_tensor("normw", [DS, 1], F32, kind="ExternalInput").ap()
    Dexp = nc.dram_tensor("Dexp", [DS, 1], F32, kind="ExternalInput").ap()
    Acol = nc.dram_tensor("Acol", [NH, 1], F32, kind="ExternalInput").ap()
    dtbc = nc.dram_tensor("dtbc", [NH, 1], F32, kind="ExternalInput").ap()
    Arow = nc.dram_tensor("Arow", [1, NH], F32, kind="ExternalInput").ap()
    ONES = nc.dram_tensor("ONES", [128, 128], F32, kind="ExternalInput").ap()
    ONESb = nc.dram_tensor("ONESb", [128, 128], BF16, kind="ExternalInput").ap()
    E127 = nc.dram_tensor("E127", [128, 1], F32, kind="ExternalInput").ap()
    SLOW = nc.dram_tensor("SLOW", [128, 128], F32, kind="ExternalInput").ap()
    UTRI = nc.dram_tensor("UTRI", [128, 128], F32, kind="ExternalInput").ap()
    MSK01 = nc.dram_tensor("MSK01", [128, 128], BF16, kind="ExternalInput").ap()
    IDNb = nc.dram_tensor("IDNb", [128, 128], BF16, kind="ExternalInput").ap()
    IDNf = nc.dram_tensor("IDNf", [128, 128], F32, kind="ExternalInput").ap()
    H128 = nc.dram_tensor("H128", [128, 128], BF16, kind="ExternalInput").ap()
    HBD = nc.dram_tensor("HBD", [128, 128], BF16, kind="ExternalInput").ap()
    out_d = nc.dram_tensor("out_d", [HID, T], F32, kind="ExternalOutput").ap()

    xBCs = nc.dram_tensor("xBCs", [CONV, T], BF16).ap()
    zsil = nc.dram_tensor("zsil", [DS, T], BF16).ap()
    dt_d = nc.dram_tensor("dt_d", [NH, T], F32).ap()
    Woutb = nc.dram_tensor("Woutb", [DS, HID], BF16).ap()
    W2pp = nc.dram_tensor("W2pp", [FFH, HID], BF16).ap()
    fh_d = nc.dram_tensor("fh_d", [FFH // 128, 128, T], BF16).ap()
    xBCp = nc.dram_tensor("xBCp", [CONV, T], BF16).ap()
    dtp = nc.dram_tensor("dtp", [NH, T], F32).ap()

    es = ExitStack()
    sem = {}
    for name in ("s_ld", "s_st", "s_pe", "s_act", "s_dve"):
        sem[name] = es.enter_context(nc.semaphore(name))
    sem_of = {"sync": sem["s_ld"], "gpsimd": sem["s_st"], "tensor": sem["s_pe"],
              "scalar": sem["s_act"], "vector": sem["s_dve"]}

    def sb(name, p, f, dt):
        return es.enter_context(nc.sbuf_tensor(name, [p, f], dt))

    def ps(name, p, f, dt=F32):
        return es.enter_context(nc.psum_tensor(name, [p, f], dt))

    cArow = sb("cArow", 1, NH, F32)
    cONESb = sb("cONESb", 128, 128, BF16)
    cONES = sb("cONES", 128, 128, F32)
    cE127 = sb("cE127", 128, 1, F32)
    cSLOW = sb("cSLOW", 128, 128, F32)
    cUTRI = sb("cUTRI", 128, 128, F32)
    cMSK = sb("cMSK", 128, 128, BF16)
    cIDb = sb("cIDb", 128, 128, BF16)
    cIDf = sb("cIDf", 128, 128, F32)
    cH128 = sb("cH128", 128, 128, BF16)
    cHBD = sb("cHBD", 128, 128, BF16)
    cMUP = sb("cMUP", 128, 67, F32)
    cCW = sb("cCW", 128, 34 * KCONV, F32)
    cCB = sb("cCB", 128, 34, F32)
    cNW = sb("cNW", 128, 32, F32)
    cDX = sb("cDX", 128, 32, F32)
    cA = sb("cA", 64, 1, F32)
    cDTB = sb("cDTB", 64, 1, F32)
    cNEA = sb("cNEA", 64, 1, F32)
    cMASKC = sb("cMASKC", 128, 1, F32)

    for d, s in ((cONES, ONES), (cONESb, ONESb), (cArow, Arow),
                 (cE127, E127), (cSLOW, SLOW), (cUTRI, UTRI),
                 (cMSK, MSK01), (cIDb, IDNb), (cIDf, IDNf), (cH128, H128),
                 (cHBD, HBD), (cA, Acol), (cDTB, dtbc), (cMASKC, maskc)):
        dma_in(d[:, :], s[:, :])
    dma_in(cMUP[:, :].rearrange("p (ft c) -> p ft c", c=1), mupc.rearrange("(ft p) c -> p ft c", p=128), slow=True)
    dma_in(cCW[:, :].rearrange("p (ft k) -> p ft k", k=4), convw.rearrange("(ft p) k -> p ft k", p=128), slow=True)
    dma_in(cCB[:, :].rearrange("p (ft c) -> p ft c", c=1), convb.rearrange("(ft p) c -> p ft c", p=128), slow=True)
    dma_in(cNW[:, :].rearrange("p (ft c) -> p ft c", c=1), normw.rearrange("(ft p) c -> p ft c", p=128), slow=True)
    dma_in(cDX[:, :].rearrange("p (ft c) -> p ft c", c=1), Dexp.rearrange("(ft p) c -> p ft c", p=128), slow=True)
    act(cNEA[:, :], cA[:, :], AF.Exp)
    ts(cNEA[:, :], cNEA[:, :], -1.0, OP.mult)

    with nc.sbuf_tensor("p0a", [128, HID], F32) as p0a, \
         nc.sbuf_tensor("p0b", [128, HID], BF16) as p0b, \
         nc.sbuf_tensor("p0o", [128, HID], BF16) as p0o, \
         nc.psum_tensor("p0ps", [128, 512], F32) as p0ps:
        W2v = W2pp.rearrange("(a g b) d -> g a (b d)", g=2, b=64)
        for b0 in range(64):
            dma_in(p0a[:, :], W2q[b0 * 128:(b0 + 1) * 128, :])
            act(p0b[:, :], p0a[:, :], AF.Copy)
            for c in range(4):
                mm(p0ps[:, :], cHBD[:, :], p0b[:, c * 512:(c + 1) * 512])
                act(p0o[:, c * 512:(c + 1) * 512], p0ps[:, :], AF.Copy)
            dma_out(W2v[:, :, b0 * HID:(b0 + 1) * HID], p0o[:, :])

    with nc.sbuf_tensor("xbf", [128, 16 * TEXT], BF16) as xbf, \
         nc.sbuf_tensor("xbf2", [128, 16 * TEXT], BF16) as xbf2, \
         nc.sbuf_tensor("wf32", [128, HID], F32) as wf32, \
         nc.sbuf_tensor("wbf", [128, HID], BF16) as wbf, \
         nc.sbuf_tensor("pbuf", [128, TEXT], F32) as pbuf, \
         nc.sbuf_tensor("obuf", [128, T], BF16) as obuf, \
         nc.sbuf_tensor("aux", [128, TEXT], F32) as aux, \
         nc.psum_tensor("aps", [128, 512], F32) as aps:
        for kt in range(16):
            dma_in(aux[:, 0:TEXT], x_fm[kt * 128:(kt + 1) * 128, :])
            act(xbf[:, kt * TEXT:(kt + 1) * TEXT], aux[:, 0:TEXT], AF.Copy)
            dma_in(aux[:, 0:TEXT], x_pfm[kt * 128:(kt + 1) * 128, :])
            act(xbf2[:, kt * TEXT:(kt + 1) * TEXT], aux[:, 0:TEXT], AF.Copy)

        def load_win(c0, w):
            dma_in(wf32[:, 0:16 * w].rearrange("p (kt c) -> p kt c", kt=16),
                   Win[:, c0:c0 + w].rearrange("(kt p) c -> p kt c", p=128))
            act(wbf[:, 0:16 * w], wf32[:, 0:16 * w], AF.Copy)

        for ft in range(34):
            c0 = DS + ft * 128
            gmt = c0 // 128
            load_win(c0, 128)
            for xsrc, dst, masked in ((xbf2, xBCp, True), (xbf, xBCs, False)):
                for tc in range(5):
                    t0 = tc * 512
                    w = 512 if tc < 4 else TEXT - 4 * 512
                    for kt in range(16):
                        mm(aps[:, 0:w], wbf[:, kt * 128:(kt + 1) * 128],
                           xsrc[:, kt * TEXT + t0: kt * TEXT + t0 + w],
                           start=(kt == 0), stop=(kt == 15))
                    ts(pbuf[:, t0:t0 + w], aps[:, 0:w], cMUP[:, gmt:gmt + 1], OP.mult)
                ts(aux[:, 0:T], pbuf[:, 125:125 + T], cCW[:, ft * 4:ft * 4 + 1],
                   OP.mult, cCB[:, ft:ft + 1], OP.add)
                for k in range(1, 4):
                    stt(aux[:, 0:T], pbuf[:, 125 + k:125 + k + T],
                        cCW[:, ft * 4 + k:ft * 4 + k + 1], aux[:, 0:T], OP.mult, OP.add)
                act(obuf[:, :], aux[:, 0:T], AF.Silu)
                if masked:
                    ts(obuf[:, :], obuf[:, :], cMASKC[:, :], OP.mult)
                dma_out(dst[ft * 128:(ft + 1) * 128, :], obuf[:, :])
        dma_in(wf32[:, 0:16 * 64].rearrange("p (kt c) -> p kt c", kt=16),
               Win[:, PROJ - 64:PROJ].rearrange("(kt p) c -> p kt c", p=128))
        act(wbf[:, 0:16 * 64], wf32[:, 0:16 * 64], AF.Copy)
        for xsrc, dst in ((xbf2, dtp), (xbf, dt_d)):
            for tc in range(4):
                t0 = 128 + tc * 512
                for kt in range(16):
                    mm(aps[0:64, 0:512], wbf[:, kt * 64:(kt + 1) * 64],
                       xsrc[:, kt * TEXT + t0: kt * TEXT + t0 + 512],
                       start=(kt == 0), stop=(kt == 15))
                ts(aux[0:64, 0:512], aps[0:64, 0:512], cMUP[0:64, 66:67], OP.mult)
                dma_out(dst[:, tc * 512:(tc + 1) * 512], aux[0:64, 0:512])
        for ft in range(32):
            c0 = ft * 128
            load_win(c0, 128)
            for tc in range(4):
                t0 = 128 + tc * 512
                for kt in range(16):
                    mm(aps[:, 0:512], wbf[:, kt * 128:(kt + 1) * 128],
                       xbf[:, kt * TEXT + t0: kt * TEXT + t0 + 512],
                       start=(kt == 0), stop=(kt == 15))
                ts(aux[:, tc * 512:(tc + 1) * 512], aps[:, 0:512],
                   cMUP[:, ft:ft + 1], OP.mult)
            act(obuf[:, :], aux[:, 0:T], AF.Silu)
            dma_out(zsil[ft * 128:(ft + 1) * 128, :], obuf[:, :])
        for kt in range(32):
            dma_in(aux[:, 0:HID], Wout[kt * 128:(kt + 1) * 128, :])
            act(obuf[:, 0:HID], aux[:, 0:HID], AF.Copy)
            dma_out(Woutb[kt * 128:(kt + 1) * 128, :], obuf[:, 0:HID])

    Y0 = sb("Y0", 128, 16 * T, BF16)
    es2 = ExitStack()
    Y1 = es2.enter_context(nc.sbuf_tensor("Y1", [128, 16 * T], BF16))

    def Ysl(ft, a, b):
        buf = Y0 if ft < 16 else Y1
        return buf[:, (ft % 16) * T + a: (ft % 16) * T + b]
    esB = ExitStack()
    xc = esB.enter_context(nc.sbuf_tensor("xc", [128, 32 * 128], BF16))
    xtm = esB.enter_context(nc.sbuf_tensor("xtm", [128, 32 * 128], BF16))
    Bfm = esB.enter_context(nc.sbuf_tensor("Bfm", [128, 128], BF16))
    Cfm = esB.enter_context(nc.sbuf_tensor("Cfm", [128, 128], BF16))
    Btm = esB.enter_context(nc.sbuf_tensor("Btm", [128, 128], BF16))
    dtr = esB.enter_context(nc.sbuf_tensor("dtr", [64, 512], F32))
    Strun = esB.enter_context(nc.sbuf_tensor("Strun", [128, NH * P], F32))
    Stbf = esB.enter_context(nc.sbuf_tensor("Stbf", [128, NH * P], BF16))
    Tst = esB.enter_context(nc.sbuf_tensor("Tst", [64, 256], F32))
    Tst2 = esB.enter_context(nc.sbuf_tensor("Tst2", [64, 256], F32))
    TM = esB.enter_context(nc.sbuf_tensor("TM", [128, 128], F32))
    ldatm = esB.enter_context(nc.sbuf_tensor("ldatm", [128, 64], F32))
    neabc = esB.enter_context(nc.sbuf_tensor("neabc", [128, 64], F32))
    eltm = esB.enter_context(nc.sbuf_tensor("eltm", [128, 64], F32))
    w2tm = esB.enter_context(nc.sbuf_tensor("w2tm", [128, 64], F32))
    Ebct = esB.enter_context(nc.sbuf_tensor("Ebct", [128, 64], F32))
    rtot = esB.enter_context(nc.sbuf_tensor("rtot", [1, 64], F32))
    Gm = esB.enter_context(nc.sbuf_tensor("Gm", [128, 128], BF16))
    rhsD = esB.enter_context(nc.sbuf_tensor("rhsD", [128, 512], F32))
    expD = esB.enter_context(nc.sbuf_tensor("expD", [128, 512], BF16))
    Mh = esB.enter_context(nc.sbuf_tensor("Mh", [128, 128], BF16))
    Vb = esB.enter_context(nc.sbuf_tensor("Vb", [128, 256], BF16))
    tmpb = esB.enter_context(nc.sbuf_tensor("tmpb", [128, 256], BF16))
    sc1 = esB.enter_context(nc.sbuf_tensor("sc1", [128, 64], F32))
    trp = esB.enter_context(nc.psum_tensor("trp", [128, 512], BF16))
    gps = esB.enter_context(nc.psum_tensor("gps", [128, 128], F32))
    dps = esB.enter_context(nc.psum_tensor("dps", [128, 512], F32))
    yps = esB.enter_context(nc.psum_tensor("yps", [128, 128], F32))
    ccps = esB.enter_context(nc.psum_tensor("ccps", [128, 256], F32))
    stps = esB.enter_context(nc.psum_tensor("stps", [128, 256], F32))
    bps = esB.enter_context(nc.psum_tensor("bps", [128, 128], F32))
    tsp = esB.enter_context(nc.psum_tensor("tsp", [128, 128], F32))
    if True:
        memset(Strun[:, :], 0.0)
        mm(bps[:, 0:64], cONES[0:1, :], cArow[:, :])
        act(neabc[:, :], bps[:, 0:64], AF.Exp)
        ts(neabc[:, :], neabc[:, :], -1.0, OP.mult)
        for prev in (True, False):
            xBCsrc = xBCp if prev else xBCs
            dtsrc = dtp if prev else dt_d
            for j in range(NCH):
                t0 = j * 128
                if j % 4 == 0:
                    dma_in(dtr[:, :], dtsrc[:, t0:t0 + 512])
                dma_in(xc[:, :].rearrange("p (ft t) -> p ft t", ft=32),
                       xBCsrc[0:DS, t0:t0 + 128].rearrange("(ft p) t -> p ft t", p=128))
                dma_in(Bfm[:, :], xBCsrc[DS:DS + 128, t0:t0 + 128])
                if not prev:
                    dma_in(Cfm[:, :], xBCsrc[DS + 128:DS + 256, t0:t0 + 128])
                for ft in range(32):
                    petr(trp[:, (ft % 4) * 128:(ft % 4) * 128 + 128],
                         xc[:, ft * 128:(ft + 1) * 128], cIDb[:, :])
                    if ft % 4 == 3:
                        act(xtm[:, (ft - 3) * 128:(ft + 1) * 128], trp[:, :], AF.Copy)
                petr(trp[:, 0:128], Bfm[:, :], cIDb[:, :])
                act(Btm[:, :], trp[:, 0:128], AF.Copy)
                dc = dtr[:, (t0 % 512):(t0 % 512) + 128]
                u1 = Tst[0:64, 0:128]
                u2 = Tst2[0:64, 0:128]
                spb = Tst2[0:64, 128:256]
                csb = Tst[0:64, 128:256]
                ts(u1, dc, cDTB[:, :], OP.add)
                act(u2, u1, AF.Abs)
                act(u2, u2, AF.Exp, scale=-1.0)
                act(u2, u2, AF.Ln, bias=1.0)
                ts(u1, u1, 0.0, OP.max)
                tt(spb, u1, u2, OP.add)
                ts(u1, spb, cNEA[:, :], OP.mult)
                ttscan(csb, cONES[0:64, :], u1, 0.0, OP.mult, OP.add)
                petr(tsp[:, 0:64], csb, cIDf[0:64, 0:64])
                act(TM[:, 0:64], tsp[:, 0:64], AF.Copy)
                petr(tsp[:, 64:128], spb, cIDf[0:64, 0:64])
                act(TM[:, 64:128], tsp[:, 64:128], AF.Copy)
                cs_tm = TM[:, 0:64]
                dt_tm = TM[:, 64:128]
                mm(bps[0:1, 0:64], cE127[:, :], cs_tm)
                act(rtot[:, :], bps[0:1, 0:64], AF.Copy)
                mm(bps[:, 0:64], cONES[0:1, :], rtot[:, :])
                act(Ebct[:, :], bps[:, 0:64], AF.Exp)
                tt(sc1[:, :], bps[:, 0:64], cs_tm, OP.subtract)
                act(sc1[:, :], sc1[:, :], AF.Exp)
                tt(w2tm[:, :], sc1[:, :], dt_tm, OP.mult)
                if not prev:
                    tt(ldatm[:, :], dt_tm, neabc[:, :], OP.mult)
                    act(eltm[:, :], cs_tm, AF.Exp)
                    mm(gps[:, :], Bfm[:, :], Cfm[:, :])
                    tt(Gm[:, :], gps[:, :], cMSK[:, :], OP.mult)
                    act(Stbf[:, :], Strun[:, :], AF.Copy)
                for g in range(16):
                    h0 = 4 * g
                    if not prev:
                        for hh in range(4):
                            ts(rhsD[:, hh * 128:(hh + 1) * 128], cUTRI[:, :],
                               ldatm[:, h0 + hh:h0 + hh + 1], OP.mult)
                        mm(dps[:, :], cSLOW[:, :], rhsD[:, :])
                        act(expD[:, :], dps[:, :], AF.Exp)
                        mm(ccps[:, :], Cfm[:, :], Stbf[:, h0 * 64:(h0 + 4) * 64])
                    for hh in range(4):
                        h = h0 + hh
                        if not prev:
                            stt(Mh[:, :], Gm[:, :], dt_tm[:, h:h + 1],
                                expD[:, hh * 128:(hh + 1) * 128], OP.mult, OP.mult)
                            mm(yps[(hh % 2) * 64:(hh % 2) * 64 + 64, :],
                               xtm[:, h * 64:(h + 1) * 64], Mh[:, :],
                               start=True, stop=False)
                            ts(tmpb[:, hh * 64:(hh + 1) * 64],
                               ccps[:, hh * 64:(hh + 1) * 64],
                               eltm[:, h:h + 1], OP.mult)
                            mm(yps[(hh % 2) * 64:(hh % 2) * 64 + 64, :],
                               tmpb[:, hh * 64:(hh + 1) * 64], cIDb[:, :],
                               start=False, stop=True)
                        ts(Vb[:, hh * 64:(hh + 1) * 64], xtm[:, h * 64:(h + 1) * 64],
                           w2tm[:, h:h + 1], OP.mult)
                        if (not prev) and hh % 2 == 1:
                            ftile = h // 2
                            stt(Ysl(ftile, t0, t0 + 128),
                                xc[:, ftile * 128:(ftile + 1) * 128],
                                cDX[:, ftile:ftile + 1], yps[:, :], OP.mult, OP.add)
                    mm(stps[:, :], Btm[:, :], Vb[:, :])
                    for hh in range(4):
                        h = h0 + hh
                        stt(Strun[:, h * 64:(h + 1) * 64],
                            Strun[:, h * 64:(h + 1) * 64],
                            Ebct[:, h:h + 1], stps[:, hh * 64:(hh + 1) * 64],
                            OP.mult, OP.add)

    esB.close()

    with nc.sbuf_tensor("zb", [128, T], BF16) as zb, \
         nc.sbuf_tensor("sqt", [128, 512], F32) as sqt, \
         nc.sbuf_tensor("rrow", [1, T], F32) as rrow, \
         nc.psum_tensor("sq0", [1, 512], F32) as sq0, \
         nc.psum_tensor("sq1", [1, 512], F32) as sq1, \
         nc.psum_tensor("sq2", [1, 512], F32) as sq2, \
         nc.psum_tensor("sq3", [1, 512], F32) as sq3, \
         nc.psum_tensor("rbc", [128, 512], F32) as rbc:
        sqp = [sq0, sq1, sq2, sq3]
        for ft in range(32):
            dma_in(zb[:, :], zsil[ft * 128:(ft + 1) * 128, :])
            ysl = Ysl(ft, 0, T)
            tt(ysl, ysl, zb[:, :], OP.mult)
            for tc in range(4):
                act(sqt[:, :], Ysl(ft, tc * 512, (tc + 1) * 512), AF.Square)
                mm(sqp[tc][:, :], cONES[:, 0:1], sqt[:, :],
                   start=(ft == 0), stop=(ft == 31))
        for tc in range(4):
            r = rrow[:, tc * 512:(tc + 1) * 512]
            ts(r, sqp[tc][:, :], 1.0 / DS, OP.mult, 1e-5, OP.add)
            act(r, r, AF.Sqrt)
            vrecip(r, r)
        for tc in range(4):
            mm(rbc[:, :], cONES[0:1, :], rrow[:, tc * 512:(tc + 1) * 512])
            for ft in range(32):
                ysl = Ysl(ft, tc * 512, (tc + 1) * 512)
                stt(ysl, ysl, cNW[:, ft:ft + 1], rbc[:, :], OP.mult, OP.mult)

    with nc.sbuf_tensor("wob", [128, 32 * 128], BF16) as wob, \
         nc.sbuf_tensor("stg", [128, 16 * 512], F32) as stg, \
         nc.sbuf_tensor("xres", [128, 512], F32) as xres, \
         nc.sbuf_tensor("sqd", [128, 512], F32) as sqd, \
         nc.sbuf_tensor("r1", [1, 512], F32) as r1, \
         nc.sbuf_tensor("r2", [1, 512], F32) as r2, \
         nc.psum_tensor("ops_", [128, 512], F32) as ops_, \
         nc.psum_tensor("s1p", [1, 512], F32) as s1p, \
         nc.psum_tensor("s2p", [1, 512], F32) as s2p, \
         nc.psum_tensor("abc", [128, 512], F32) as abc, \
         nc.psum_tensor("bbc", [128, 512], F32) as bbc:
        for tc in range(4):
            t0 = tc * 512
            for dtile in range(16):
                dma_in(wob[:, :].rearrange("p (kt c) -> p kt c", kt=32),
                       Woutb[:, dtile * 128:(dtile + 1) * 128]
                       .rearrange("(kt p) c -> p kt c", p=128))
                for kt in range(32):
                    mm(ops_[:, :], wob[:, kt * 128:(kt + 1) * 128],
                       Ysl(kt, t0, t0 + 512),
                       start=(kt == 0), stop=(kt == 31))
                dma_in(xres[:, :], x_fm[dtile * 128:(dtile + 1) * 128,
                                        128 + t0:128 + t0 + 512])
                tt(stg[:, dtile * 512:(dtile + 1) * 512], ops_[:, :], xres[:, :],
                   OP.add)
                ssl = stg[:, dtile * 512:(dtile + 1) * 512]
                mm(s1p[:, :], cONES[:, 0:1], ssl,
                   start=(dtile == 0), stop=(dtile == 15))
                act(sqd[:, :], ssl, AF.Square)
                mm(s2p[:, :], cONES[:, 0:1], sqd[:, :],
                   start=(dtile == 0), stop=(dtile == 15))
            ts(r1[:, :], s1p[:, :], 1.0 / HID, OP.mult)
            act(r2[:, :], r1[:, :], AF.Square)
            ts(r2[:, :], r2[:, :], -1.0, OP.mult)
            stt(r2[:, :], s2p[:, :], 1.0 / HID, r2[:, :], OP.mult, OP.add)
            ts(r2[:, :], r2[:, :], 1e-5, OP.add)
            act(r2[:, :], r2[:, :], AF.Sqrt)
            vrecip(r2[:, :], r2[:, :])
            tt(r1[:, :], r1[:, :], r2[:, :], OP.mult)
            ts(r1[:, :], r1[:, :], -1.0, OP.mult)
            mm(abc[:, :], cONES[0:1, :], r2[:, :])
            mm(bbc[:, :], cONES[0:1, :], r1[:, :])
            for dtile in range(16):
                ssl = stg[:, dtile * 512:(dtile + 1) * 512]
                tt(ssl, ssl, abc[:, :], OP.mult)
                tt(Y0[:, dtile * T + t0: dtile * T + t0 + 512], ssl, bbc[:, :],
                   OP.add)

    es2.close()

    with nc.sbuf_tensor("w13f", [128, 4096], F32) as w13f, \
         nc.sbuf_tensor("w13b", [128, 2 * 4096], BF16) as w13b, \
         nc.sbuf_tensor("silb", [128, 512], F32) as silb, \
         nc.sbuf_tensor("hidb", [128, 2 * T], BF16) as hidb, \
         nc.sbuf_tensor("fhb", [128, T], BF16) as fhb, \
         nc.psum_tensor("h1p", [128, 512], F32) as h1p, \
         nc.psum_tensor("h3p", [128, 512], F32) as h3p, \
         nc.psum_tensor("fhp", [128, 512], F32) as fhp:
        for hc in range(32):
            c0 = hc * 256
            dma_in(w13f[:, 0:4096].rearrange("p (kt c) -> p kt c", kt=16),
                   W1T[:, c0:c0 + 256].rearrange("(kt p) c -> p kt c", p=128))
            act(w13b[:, 0:4096], w13f[:, 0:4096], AF.Copy)
            dma_in(w13f[:, 0:4096].rearrange("p (kt c) -> p kt c", kt=16),
                   W3T[:, c0:c0 + 256].rearrange("(kt p) c -> p kt c", p=128))
            act(w13b[:, 4096:8192], w13f[:, 0:4096], AF.Copy)
            for hft in range(2):
                for tc in range(4):
                    t0 = tc * 512
                    for kt in range(16):
                        mm(h1p[:, :], w13b[:, kt * 256 + hft * 128: kt * 256 + hft * 128 + 128],
                           Y0[:, kt * T + t0: kt * T + t0 + 512],
                           start=(kt == 0), stop=(kt == 15))
                    for kt in range(16):
                        mm(h3p[:, :], w13b[:, 4096 + kt * 256 + hft * 128: 4096 + kt * 256 + hft * 128 + 128],
                           Y0[:, kt * T + t0: kt * T + t0 + 512],
                           start=(kt == 0), stop=(kt == 15))
                    act(silb[:, :], h1p[:, :], AF.Silu)
                    tt(hidb[:, hft * T + t0: hft * T + t0 + 512], silb[:, :],
                       h3p[:, :], OP.mult)
                for tc in range(4):
                    t0 = tc * 512
                    mm(fhp[:, :], cH128[:, :], hidb[:, hft * T + t0: hft * T + t0 + 512])
                    act(fhb[:, t0:t0 + 512], fhp[:, :], AF.Copy, scale=RSQ)
                dma_out(fh_d[hc * 2 + hft, :, :], fhb[:, :])

    with nc.sbuf_tensor("fhs", [128, 64 * 512], BF16) as fhs, \
         nc.sbuf_tensor("w2b", [128, 64 * 128], BF16) as w2b, \
         nc.sbuf_tensor("stg2", [128, 16 * 512], BF16) as stg2, \
         nc.sbuf_tensor("x1s", [128, 512], BF16) as x1s, \
         nc.sbuf_tensor("sq2b", [128, 512], F32) as sq2b, \
         nc.sbuf_tensor("ro1", [1, 512], F32) as ro1, \
         nc.sbuf_tensor("ro2", [1, 512], F32) as ro2, \
         nc.sbuf_tensor("oux", [128, 512], F32) as oux, \
         nc.psum_tensor("fps", [128, 512], F32) as fps, \
         nc.psum_tensor("t1p", [1, 512], F32) as t1p, \
         nc.psum_tensor("t2p", [1, 512], F32) as t2p, \
         nc.psum_tensor("ab2", [128, 512], F32) as ab2, \
         nc.psum_tensor("bb2", [128, 512], F32) as bb2:
        for tc in range(4):
            t0 = tc * 512
            for kt in range(64):
                dma_in(fhs[:, kt * 512:(kt + 1) * 512], fh_d[kt, :, t0:t0 + 512])
            for dtile in range(16):
                dma_in(w2b[:, :].rearrange("p (kt c) -> p kt c", kt=64),
                       W2pp[:, dtile * 128:(dtile + 1) * 128]
                       .rearrange("(kt p) c -> p kt c", p=128))
                for kt in range(64):
                    mm(fps[:, :], w2b[:, kt * 128:(kt + 1) * 128],
                       fhs[:, kt * 512:(kt + 1) * 512],
                       start=(kt == 0), stop=(kt == 63))
                ssl = stg2[:, dtile * 512:(dtile + 1) * 512]
                tt(ssl, fps[:, :], Y0[:, dtile * T + t0: dtile * T + t0 + 512],
                   OP.add)
                mm(t1p[:, :], cONESb[:, 0:1], ssl,
                   start=(dtile == 0), stop=(dtile == 15))
                act(sq2b[:, :], ssl, AF.Square)
                mm(t2p[:, :], cONES[:, 0:1], sq2b[:, :],
                   start=(dtile == 0), stop=(dtile == 15))
            ts(ro1[:, :], t1p[:, :], 1.0 / HID, OP.mult)
            act(ro2[:, :], ro1[:, :], AF.Square)
            ts(ro2[:, :], ro2[:, :], -1.0, OP.mult)
            stt(ro2[:, :], t2p[:, :], 1.0 / HID, ro2[:, :], OP.mult, OP.add)
            ts(ro2[:, :], ro2[:, :], 1e-5, OP.add)
            act(ro2[:, :], ro2[:, :], AF.Sqrt)
            vrecip(ro2[:, :], ro2[:, :])
            tt(ro1[:, :], ro1[:, :], ro2[:, :], OP.mult)
            ts(ro1[:, :], ro1[:, :], -1.0, OP.mult)
            mm(ab2[:, :], cONES[0:1, :], ro2[:, :])
            mm(bb2[:, :], cONES[0:1, :], ro1[:, :])
            for dtile in range(16):
                ssl = stg2[:, dtile * 512:(dtile + 1) * 512]
                tt(oux[:, :], ssl, ab2[:, :], OP.mult)
                tt(oux[:, :], oux[:, :], bb2[:, :], OP.add)
                dma_out(out_d[dtile * 128:(dtile + 1) * 128, t0:t0 + 512],
                        oux[:, :])

    ops = E.ops
    totals = dict(E.counts)
    with nc.Block() as block:
        def section(eng_name):
            def body(eng):
                cnt = {e: 0 for e in ENG}
                for (engine, fn, inc, prev) in ops:
                    if engine == eng_name:
                        if prev is not None:
                            pe_, pc_ = prev
                            eng.wait_ge(sem_of[pe_], pc_)
                        instr = fn(eng)
                        instr.then_inc(sem_of[engine], inc)
                    cnt[engine] += inc
                if eng_name == "gpsimd":
                    eng.wait_ge(sem_of["gpsimd"], totals["gpsimd"])
            return body
        block.sync(section("sync"))
        block.tensor(section("tensor"))
        block.vector(section("vector"))
        block.scalar(section("scalar"))
        block.gpsimd(section("gpsimd"))

    es.close()
    return nc


def _consts():
    H = np.array([[1.0]], dtype=np.float32)
    while H.shape[0] < 128:
        H = np.block([[H, H], [H, -H]])
    H64 = H[:64, :64].copy()
    HBDm = np.zeros((128, 128), np.float32)
    HBDm[:64, :64] = H64
    HBDm[64:, 64:] = H64
    r = np.arange(128)
    SL = (r[:, None] > r[None, :]).astype(np.float32)
    UT = (r[:, None] <= r[None, :]).astype(np.float32)
    MS = (r[None, :] >= r[:, None]).astype(np.float32)
    IDN = np.eye(128, dtype=np.float32)
    E127v = np.zeros((128, 1), np.float32)
    E127v[127, 0] = 1.0
    return H, H64, HBDm, SL, UT, MS, IDN, E127v


def kernel(L0_attn_hidden_states, L0_attn_residual, L0_attn_W_in, L0_attn_conv_w,
           L0_attn_conv_b, L0_attn_A_log, L0_attn_D, L0_attn_dt_bias,
           L0_attn_norm_w, L0_attn_W_out, L0_attn_mup_vector,
           L0_ffn_W1, L0_ffn_W2, L0_ffn_W3):
    bf = ml_dtypes.bfloat16
    x = np.asarray(L0_attn_hidden_states, np.float32)
    B, L, _ = x.shape
    H, H64, HBDm, SL, UT, MS, IDN, E127v = _consts()

    W2T = np.ascontiguousarray(np.asarray(L0_ffn_W2, np.float32).T)
    W2q = np.ascontiguousarray(
        W2T.reshape(64, 2, 64, HID).transpose(2, 1, 0, 3)).reshape(FFH, HID)

    shared = {
        "Win": np.ascontiguousarray(np.asarray(L0_attn_W_in, np.float32)),
        "Wout": np.ascontiguousarray(np.asarray(L0_attn_W_out, np.float32)),
        "W1T": np.ascontiguousarray(np.asarray(L0_ffn_W1, np.float32).T),
        "W3T": np.ascontiguousarray(np.asarray(L0_ffn_W3, np.float32).T),
        "W2q": W2q,
        "convw": np.ascontiguousarray(np.asarray(L0_attn_conv_w, np.float32)),
        "convb": np.asarray(L0_attn_conv_b, np.float32).reshape(CONV, 1),
        "mupc": np.concatenate([np.asarray(L0_attn_mup_vector, np.float32),
            np.zeros(67 * 128 - PROJ, np.float32)]).reshape(67 * 128, 1),
        "normw": np.asarray(L0_attn_norm_w, np.float32).reshape(DS, 1),
        "Dexp": np.repeat(np.asarray(L0_attn_D, np.float32), P).reshape(DS, 1),
        "Acol": np.asarray(L0_attn_A_log, np.float32).reshape(NH, 1),
        "Arow": np.asarray(L0_attn_A_log, np.float32).reshape(1, NH),
        "dtbc": np.asarray(L0_attn_dt_bias, np.float32).reshape(NH, 1),
        "ONES": np.ones((128, 128), np.float32),
        "ONESb": np.ones((128, 128), bf),
        "E127": E127v,
        "SLOW": SL, "UTRI": UT,
        "MSK01": MS.astype(bf),
        "IDNb": IDN.astype(bf), "IDNf": IDN,
        "H128": H.astype(bf), "HBD": HBDm.astype(bf),
    }

    in_maps = []
    for c in range(8):
        b, h = c // 2, c % 2
        xT = np.ascontiguousarray(x[b].T)
        own = xT[:, h * T:(h + 1) * T]
        ext = np.zeros((HID, TEXT), np.float32)
        ext[:, 128:] = own
        if h == 1:
            ext[:, 125:128] = xT[:, T - 3:T]
        m = dict(shared)
        m["x_fm"] = ext
        if h == 1:
            pext = np.zeros((HID, TEXT), np.float32)
            pext[:, 128:] = xT[:, 0:T]
            m["x_pfm"] = pext
        else:
            m["x_pfm"] = np.zeros((HID, TEXT), np.float32)
        m["maskc"] = np.full((128, 1), float(h), np.float32)
        in_maps.append(m)

    nc = build_nc()
    res = run_bass_kernel_spmd(nc, in_maps, core_ids=list(range(8)))
    out = np.empty((B, L, HID), np.float32)
    for c in range(8):
        b, h = c // 2, c % 2
        out[b, h * T:(h + 1) * T, :] = res.results[c]["out_d"].T
    return out


# version 04
# speedup vs baseline: 1.4753x; 1.4753x over previous
import numpy as np
import ml_dtypes
import concourse.bass as bass
import concourse.mybir as mybir
from concourse.bass_utils import run_bass_kernel_spmd
from contextlib import ExitStack

F32 = mybir.dt.float32
BF16 = mybir.dt.bfloat16
AF = mybir.ActivationFunctionType
OP = mybir.AluOpType

HID = 2048
DS = 4096
S = 128
NH = 64
P = 64
KCONV = 4
CONV = DS + 2 * S
PROJ = 2 * DS + 2 * S + NH
T = 2048
TEXT = T + 128
NCH = T // 128
FFH = 8192
RSQ = 1.0 / np.sqrt(FFH)

ENG = ("sync", "tensor", "vector", "scalar", "gpsimd")


class Em:

    def __init__(self):
        self.ops = []
        self.counts = {e: 0 for e in ENG}
        self.last = None

    def emit(self, engine, fn, inc):
        prev = self.last
        self.counts[engine] += inc
        self.ops.append((engine, fn, inc, prev))
        self.last = (engine, self.counts[engine])


E = None


def dma_in(dst, src, slow=False):
    E.emit("sync", lambda e, d=dst, s=src, sl=slow: e.dma_start(
        out=d, in_=s, allow_slow_non_contiguous=sl) if sl else e.dma_start(out=d, in_=s), 16)


def dma_out(dst, src, slow=False):
    E.emit("gpsimd", lambda e, d=dst, s=src, sl=slow: e.dma_start(
        out=d, in_=s, allow_slow_non_contiguous=sl) if sl else e.dma_start(out=d, in_=s), 16)


def mm(out, lhsT, rhs, start=True, stop=True):
    E.emit("tensor", lambda e, o=out, l=lhsT, r=rhs, st=start, sp=stop:
           e.matmul(o, l, r, start=st, stop=sp), 1)


def petr(out, in_, idn):
    E.emit("tensor", lambda e, o=out, i=in_, d=idn: e.transpose(o, i, d), 1)


def act(out, in_, func, bias=0.0, scale=1.0):
    E.emit("scalar", lambda e, o=out, i=in_, f=func, b=bias, s=scale:
           e.activation(o, i, f, bias=b, scale=s), 1)


def tt(out, in0, in1, op):
    E.emit("vector", lambda e, o=out, a=in0, b=in1, p=op: e.tensor_tensor(o, a, b, p), 1)


def ts(out, in0, s1, op0, s2=None, op1=OP.bypass):
    E.emit("vector", lambda e, o=out, a=in0, x=s1, y=s2, p0=op0, p1=op1:
           e.tensor_scalar(o, a, x, y, p0, p1), 1)


def stt(out, in0, scalar, in1, op0, op1):
    E.emit("vector", lambda e, o=out, a=in0, s=scalar, b=in1, p0=op0, p1=op1:
           e.scalar_tensor_tensor(o, a, s, b, p0, p1), 1)


def ttscan(out, d0, d1, init, op0, op1):
    E.emit("vector", lambda e, o=out, a=d0, b=d1, i=init, p0=op0, p1=op1:
           e.tensor_tensor_scan(o, a, b, i, p0, p1), 1)


def vrecip(out, in_):
    E.emit("vector", lambda e, o=out, i=in_: e.reciprocal(o, i), 1)


def memset(ap, v):
    E.emit("vector", lambda e, a=ap, c=v: e.memset(a, c), 1)


def build_nc():
    global E
    E = Em()
    nc = bass.Bass("TRN2", target_bir_lowering=False, debug=False)

    x_fm = nc.dram_tensor("x_fm", [HID, TEXT], F32, kind="ExternalInput").ap()
    x_pfm = nc.dram_tensor("x_pfm", [HID, TEXT], F32, kind="ExternalInput").ap()
    maskc = nc.dram_tensor("maskc", [128, 1], F32, kind="ExternalInput").ap()
    Win = nc.dram_tensor("Win", [HID, PROJ], F32, kind="ExternalInput").ap()
    Wout = nc.dram_tensor("Wout", [DS, HID], F32, kind="ExternalInput").ap()
    W1T = nc.dram_tensor("W1T", [HID, FFH], F32, kind="ExternalInput").ap()
    W3T = nc.dram_tensor("W3T", [HID, FFH], F32, kind="ExternalInput").ap()
    W2q = nc.dram_tensor("W2q", [FFH, HID], F32, kind="ExternalInput").ap()
    convw = nc.dram_tensor("convw", [CONV, KCONV], F32, kind="ExternalInput").ap()
    convb = nc.dram_tensor("convb", [CONV, 1], F32, kind="ExternalInput").ap()
    mupc = nc.dram_tensor("mupc", [67 * 128, 1], F32, kind="ExternalInput").ap()
    normw = nc.dram_tensor("normw", [DS, 1], F32, kind="ExternalInput").ap()
    Dexp = nc.dram_tensor("Dexp", [DS, 1], F32, kind="ExternalInput").ap()
    Acol = nc.dram_tensor("Acol", [NH, 1], F32, kind="ExternalInput").ap()
    dtbc = nc.dram_tensor("dtbc", [NH, 1], F32, kind="ExternalInput").ap()
    Arow = nc.dram_tensor("Arow", [1, NH], F32, kind="ExternalInput").ap()
    ONES = nc.dram_tensor("ONES", [128, 128], F32, kind="ExternalInput").ap()
    ONESb = nc.dram_tensor("ONESb", [128, 128], BF16, kind="ExternalInput").ap()
    E127 = nc.dram_tensor("E127", [128, 1], F32, kind="ExternalInput").ap()
    SLOW = nc.dram_tensor("SLOW", [128, 128], F32, kind="ExternalInput").ap()
    UTRI = nc.dram_tensor("UTRI", [128, 128], F32, kind="ExternalInput").ap()
    MSK01 = nc.dram_tensor("MSK01", [128, 128], BF16, kind="ExternalInput").ap()
    IDNb = nc.dram_tensor("IDNb", [128, 128], BF16, kind="ExternalInput").ap()
    IDNf = nc.dram_tensor("IDNf", [128, 128], F32, kind="ExternalInput").ap()
    H128 = nc.dram_tensor("H128", [128, 128], BF16, kind="ExternalInput").ap()
    HBD = nc.dram_tensor("HBD", [128, 128], BF16, kind="ExternalInput").ap()
    out_d = nc.dram_tensor("out_d", [HID, T], F32, kind="ExternalOutput").ap()

    xBCs = nc.dram_tensor("xBCs", [CONV, T], BF16).ap()
    zsil = nc.dram_tensor("zsil", [DS, T], BF16).ap()
    dt_d = nc.dram_tensor("dt_d", [NH, T], F32).ap()
    Woutb = nc.dram_tensor("Woutb", [DS, HID], BF16).ap()
    W2pp = nc.dram_tensor("W2pp", [FFH, HID], BF16).ap()
    fh_d = nc.dram_tensor("fh_d", [FFH // 128, 128, T], BF16).ap()
    xBCp = nc.dram_tensor("xBCp", [CONV, T], BF16).ap()
    dtp = nc.dram_tensor("dtp", [NH, T], F32).ap()

    es = ExitStack()
    sem = {}
    for name in ("s_ld", "s_st", "s_pe", "s_act", "s_dve"):
        sem[name] = es.enter_context(nc.semaphore(name))
    sem_of = {"sync": sem["s_ld"], "gpsimd": sem["s_st"], "tensor": sem["s_pe"],
              "scalar": sem["s_act"], "vector": sem["s_dve"]}

    def sb(name, p, f, dt):
        return es.enter_context(nc.sbuf_tensor(name, [p, f], dt))

    def ps(name, p, f, dt=F32):
        return es.enter_context(nc.psum_tensor(name, [p, f], dt))

    cArow = sb("cArow", 1, NH, F32)
    cONESb = sb("cONESb", 128, 128, BF16)
    cONES = sb("cONES", 128, 128, F32)
    cE127 = sb("cE127", 128, 1, F32)
    cSLOW = sb("cSLOW", 128, 128, F32)
    cUTRI = sb("cUTRI", 128, 128, F32)
    cMSK = sb("cMSK", 128, 128, BF16)
    cIDb = sb("cIDb", 128, 128, BF16)
    cIDf = sb("cIDf", 128, 128, F32)
    cH128 = sb("cH128", 128, 128, BF16)
    cHBD = sb("cHBD", 128, 128, BF16)
    cMUP = sb("cMUP", 128, 67, F32)
    cCW = sb("cCW", 128, 34 * KCONV, F32)
    cCB = sb("cCB", 128, 34, F32)
    cNW = sb("cNW", 128, 32, F32)
    cDX = sb("cDX", 128, 32, F32)
    cA = sb("cA", 64, 1, F32)
    cDTB = sb("cDTB", 64, 1, F32)
    cNEA = sb("cNEA", 64, 1, F32)
    cMASKC = sb("cMASKC", 128, 1, F32)

    for d, s in ((cONES, ONES), (cONESb, ONESb), (cArow, Arow),
                 (cE127, E127), (cSLOW, SLOW), (cUTRI, UTRI),
                 (cMSK, MSK01), (cIDb, IDNb), (cIDf, IDNf), (cH128, H128),
                 (cHBD, HBD), (cA, Acol), (cDTB, dtbc), (cMASKC, maskc)):
        dma_in(d[:, :], s[:, :])
    dma_in(cMUP[:, :].rearrange("p (ft c) -> p ft c", c=1), mupc.rearrange("(ft p) c -> p ft c", p=128), slow=True)
    dma_in(cCW[:, :].rearrange("p (ft k) -> p ft k", k=4), convw.rearrange("(ft p) k -> p ft k", p=128), slow=True)
    dma_in(cCB[:, :].rearrange("p (ft c) -> p ft c", c=1), convb.rearrange("(ft p) c -> p ft c", p=128), slow=True)
    dma_in(cNW[:, :].rearrange("p (ft c) -> p ft c", c=1), normw.rearrange("(ft p) c -> p ft c", p=128), slow=True)
    dma_in(cDX[:, :].rearrange("p (ft c) -> p ft c", c=1), Dexp.rearrange("(ft p) c -> p ft c", p=128), slow=True)
    act(cNEA[:, :], cA[:, :], AF.Exp)
    ts(cNEA[:, :], cNEA[:, :], -1.0, OP.mult)

    with nc.sbuf_tensor("p0a", [128, HID], F32) as p0a, \
         nc.sbuf_tensor("p0b", [128, HID], BF16) as p0b, \
         nc.sbuf_tensor("p0o", [128, HID], BF16) as p0o, \
         nc.psum_tensor("p0ps", [128, 512], F32) as p0ps:
        W2v = W2pp.rearrange("(a g b) d -> g a (b d)", g=2, b=64)
        for b0 in range(64):
            dma_in(p0a[:, :], W2q[b0 * 128:(b0 + 1) * 128, :])
            act(p0b[:, :], p0a[:, :], AF.Copy)
            for c in range(4):
                mm(p0ps[:, :], cHBD[:, :], p0b[:, c * 512:(c + 1) * 512])
                act(p0o[:, c * 512:(c + 1) * 512], p0ps[:, :], AF.Copy)
            dma_out(W2v[:, :, b0 * HID:(b0 + 1) * HID], p0o[:, :])

    with nc.sbuf_tensor("xbf", [128, 16 * TEXT], BF16) as xbf, \
         nc.sbuf_tensor("xbf2", [128, 16 * TEXT], BF16) as xbf2, \
         nc.sbuf_tensor("wf32", [128, HID], F32) as wf32, \
         nc.sbuf_tensor("wbf", [128, HID], BF16) as wbf, \
         nc.sbuf_tensor("pbuf", [128, TEXT], F32) as pbuf, \
         nc.sbuf_tensor("obuf", [128, T], BF16) as obuf, \
         nc.sbuf_tensor("aux", [128, TEXT], F32) as aux, \
         nc.psum_tensor("aps", [128, 512], F32) as aps:
        for kt in range(16):
            dma_in(aux[:, 0:TEXT], x_fm[kt * 128:(kt + 1) * 128, :])
            act(xbf[:, kt * TEXT:(kt + 1) * TEXT], aux[:, 0:TEXT], AF.Copy)
            dma_in(aux[:, 0:TEXT], x_pfm[kt * 128:(kt + 1) * 128, :])
            act(xbf2[:, kt * TEXT:(kt + 1) * TEXT], aux[:, 0:TEXT], AF.Copy)

        def load_win(c0, w):
            dma_in(wf32[:, 0:16 * w].rearrange("p (kt c) -> p kt c", kt=16),
                   Win[:, c0:c0 + w].rearrange("(kt p) c -> p kt c", p=128))
            act(wbf[:, 0:16 * w], wf32[:, 0:16 * w], AF.Copy)

        for ft in range(34):
            c0 = DS + ft * 128
            gmt = c0 // 128
            load_win(c0, 128)
            for xsrc, dst, masked in ((xbf2, xBCp, True), (xbf, xBCs, False)):
                for tc in range(5):
                    t0 = tc * 512
                    w = 512 if tc < 4 else TEXT - 4 * 512
                    for kt in range(16):
                        mm(aps[:, 0:w], wbf[:, kt * 128:(kt + 1) * 128],
                           xsrc[:, kt * TEXT + t0: kt * TEXT + t0 + w],
                           start=(kt == 0), stop=(kt == 15))
                    ts(pbuf[:, t0:t0 + w], aps[:, 0:w], cMUP[:, gmt:gmt + 1], OP.mult)
                ts(aux[:, 0:T], pbuf[:, 125:125 + T], cCW[:, ft * 4:ft * 4 + 1],
                   OP.mult, cCB[:, ft:ft + 1], OP.add)
                for k in range(1, 4):
                    stt(aux[:, 0:T], pbuf[:, 125 + k:125 + k + T],
                        cCW[:, ft * 4 + k:ft * 4 + k + 1], aux[:, 0:T], OP.mult, OP.add)
                act(obuf[:, :], aux[:, 0:T], AF.Silu)
                if masked:
                    ts(obuf[:, :], obuf[:, :], cMASKC[:, :], OP.mult)
                dma_out(dst[ft * 128:(ft + 1) * 128, :], obuf[:, :])
        dma_in(wf32[:, 0:16 * 64].rearrange("p (kt c) -> p kt c", kt=16),
               Win[:, PROJ - 64:PROJ].rearrange("(kt p) c -> p kt c", p=128))
        act(wbf[:, 0:16 * 64], wf32[:, 0:16 * 64], AF.Copy)
        for xsrc, dst in ((xbf2, dtp), (xbf, dt_d)):
            for tc in range(4):
                t0 = 128 + tc * 512
                for kt in range(16):
                    mm(aps[0:64, 0:512], wbf[:, kt * 64:(kt + 1) * 64],
                       xsrc[:, kt * TEXT + t0: kt * TEXT + t0 + 512],
                       start=(kt == 0), stop=(kt == 15))
                ts(aux[0:64, 0:512], aps[0:64, 0:512], cMUP[0:64, 66:67], OP.mult)
                dma_out(dst[:, tc * 512:(tc + 1) * 512], aux[0:64, 0:512])
        for ft in range(32):
            c0 = ft * 128
            load_win(c0, 128)
            for tc in range(4):
                t0 = 128 + tc * 512
                for kt in range(16):
                    mm(aps[:, 0:512], wbf[:, kt * 128:(kt + 1) * 128],
                       xbf[:, kt * TEXT + t0: kt * TEXT + t0 + 512],
                       start=(kt == 0), stop=(kt == 15))
                ts(aux[:, tc * 512:(tc + 1) * 512], aps[:, 0:512],
                   cMUP[:, ft:ft + 1], OP.mult)
            act(obuf[:, :], aux[:, 0:T], AF.Silu)
            dma_out(zsil[ft * 128:(ft + 1) * 128, :], obuf[:, :])
        for kt in range(32):
            dma_in(aux[:, 0:HID], Wout[kt * 128:(kt + 1) * 128, :])
            act(obuf[:, 0:HID], aux[:, 0:HID], AF.Copy)
            dma_out(Woutb[kt * 128:(kt + 1) * 128, :], obuf[:, 0:HID])

    Y0 = sb("Y0", 128, 16 * T, BF16)
    es2 = ExitStack()
    Y1 = es2.enter_context(nc.sbuf_tensor("Y1", [128, 16 * T], BF16))

    def Ysl(ft, a, b):
        buf = Y0 if ft < 16 else Y1
        return buf[:, (ft % 16) * T + a: (ft % 16) * T + b]
    esB = ExitStack()
    xc = esB.enter_context(nc.sbuf_tensor("xc", [128, 32 * 128], BF16))
    xtm = esB.enter_context(nc.sbuf_tensor("xtm", [128, 32 * 128], BF16))
    Bfm = esB.enter_context(nc.sbuf_tensor("Bfm", [128, 128], BF16))
    Cfm = esB.enter_context(nc.sbuf_tensor("Cfm", [128, 128], BF16))
    Btm = esB.enter_context(nc.sbuf_tensor("Btm", [128, 128], BF16))
    dtr = esB.enter_context(nc.sbuf_tensor("dtr", [64, 512], F32))
    Strun = esB.enter_context(nc.sbuf_tensor("Strun", [128, NH * P], F32))
    Stbf = esB.enter_context(nc.sbuf_tensor("Stbf", [128, NH * P], BF16))
    Tst = esB.enter_context(nc.sbuf_tensor("Tst", [64, 256], F32))
    Tst2 = esB.enter_context(nc.sbuf_tensor("Tst2", [64, 256], F32))
    TM = esB.enter_context(nc.sbuf_tensor("TM", [128, 128], F32))
    ldatm = esB.enter_context(nc.sbuf_tensor("ldatm", [128, 64], F32))
    neabc = esB.enter_context(nc.sbuf_tensor("neabc", [128, 64], F32))
    eltm = esB.enter_context(nc.sbuf_tensor("eltm", [128, 64], F32))
    w2tm = esB.enter_context(nc.sbuf_tensor("w2tm", [128, 64], F32))
    Ebct = esB.enter_context(nc.sbuf_tensor("Ebct", [128, 64], F32))
    rtot = esB.enter_context(nc.sbuf_tensor("rtot", [1, 64], F32))
    Gm = esB.enter_context(nc.sbuf_tensor("Gm", [128, 128], BF16))
    rhsD = esB.enter_context(nc.sbuf_tensor("rhsD", [128, 512], F32))
    expD = esB.enter_context(nc.sbuf_tensor("expD", [128, 512], BF16))
    Mh = esB.enter_context(nc.sbuf_tensor("Mh", [128, 128], BF16))
    Vb = esB.enter_context(nc.sbuf_tensor("Vb", [128, 256], BF16))
    tmpb = esB.enter_context(nc.sbuf_tensor("tmpb", [128, 256], BF16))
    sc1 = esB.enter_context(nc.sbuf_tensor("sc1", [128, 64], F32))
    trp = esB.enter_context(nc.psum_tensor("trp", [128, 512], BF16))
    gps = esB.enter_context(nc.psum_tensor("gps", [128, 128], F32))
    dps = esB.enter_context(nc.psum_tensor("dps", [128, 512], F32))
    yps = esB.enter_context(nc.psum_tensor("yps", [128, 128], F32))
    ccps = esB.enter_context(nc.psum_tensor("ccps", [128, 256], F32))
    stps = esB.enter_context(nc.psum_tensor("stps", [128, 256], F32))
    bps = esB.enter_context(nc.psum_tensor("bps", [128, 128], F32))
    tsp = esB.enter_context(nc.psum_tensor("tsp", [128, 128], F32))
    if True:
        memset(Strun[:, :], 0.0)
        mm(bps[:, 0:64], cONES[0:1, :], cArow[:, :])
        act(neabc[:, :], bps[:, 0:64], AF.Exp)
        ts(neabc[:, :], neabc[:, :], -1.0, OP.mult)
        for prev in (True, False):
            xBCsrc = xBCp if prev else xBCs
            dtsrc = dtp if prev else dt_d
            for j in range(NCH):
                t0 = j * 128
                if j % 4 == 0:
                    dma_in(dtr[:, :], dtsrc[:, t0:t0 + 512])
                dma_in(xc[:, :].rearrange("p (ft t) -> p ft t", ft=32),
                       xBCsrc[0:DS, t0:t0 + 128].rearrange("(ft p) t -> p ft t", p=128))
                dma_in(Bfm[:, :], xBCsrc[DS:DS + 128, t0:t0 + 128])
                if not prev:
                    dma_in(Cfm[:, :], xBCsrc[DS + 128:DS + 256, t0:t0 + 128])
                for ft in range(32):
                    petr(trp[:, (ft % 4) * 128:(ft % 4) * 128 + 128],
                         xc[:, ft * 128:(ft + 1) * 128], cIDb[:, :])
                    if ft % 4 == 3:
                        act(xtm[:, (ft - 3) * 128:(ft + 1) * 128], trp[:, :], AF.Copy)
                petr(trp[:, 0:128], Bfm[:, :], cIDb[:, :])
                act(Btm[:, :], trp[:, 0:128], AF.Copy)
                dc = dtr[:, (t0 % 512):(t0 % 512) + 128]
                u1 = Tst[0:64, 0:128]
                u2 = Tst2[0:64, 0:128]
                spb = Tst2[0:64, 128:256]
                csb = Tst[0:64, 128:256]
                ts(u1, dc, cDTB[:, :], OP.add)
                act(u2, u1, AF.Abs)
                act(u2, u2, AF.Exp, scale=-1.0)
                act(u2, u2, AF.Ln, bias=1.0)
                ts(u1, u1, 0.0, OP.max)
                tt(spb, u1, u2, OP.add)
                ts(u1, spb, cNEA[:, :], OP.mult)
                ttscan(csb, cONES[0:64, :], u1, 0.0, OP.mult, OP.add)
                petr(tsp[:, 0:64], csb, cIDf[0:64, 0:64])
                act(TM[:, 0:64], tsp[:, 0:64], AF.Copy)
                petr(tsp[:, 64:128], spb, cIDf[0:64, 0:64])
                act(TM[:, 64:128], tsp[:, 64:128], AF.Copy)
                cs_tm = TM[:, 0:64]
                dt_tm = TM[:, 64:128]
                mm(bps[0:1, 0:64], cE127[:, :], cs_tm)
                act(rtot[:, :], bps[0:1, 0:64], AF.Copy)
                mm(bps[:, 0:64], cONES[0:1, :], rtot[:, :])
                act(Ebct[:, :], bps[:, 0:64], AF.Exp)
                tt(sc1[:, :], bps[:, 0:64], cs_tm, OP.subtract)
                act(sc1[:, :], sc1[:, :], AF.Exp)
                tt(w2tm[:, :], sc1[:, :], dt_tm, OP.mult)
                if not prev:
                    tt(ldatm[:, :], dt_tm, neabc[:, :], OP.mult)
                    act(eltm[:, :], cs_tm, AF.Exp)
                    mm(gps[:, :], Bfm[:, :], Cfm[:, :])
                    tt(Gm[:, :], gps[:, :], cMSK[:, :], OP.mult)
                    act(Stbf[:, :], Strun[:, :], AF.Copy)
                for g in range(16):
                    h0 = 4 * g
                    if not prev:
                        for hh in range(4):
                            ts(rhsD[:, hh * 128:(hh + 1) * 128], cUTRI[:, :],
                               ldatm[:, h0 + hh:h0 + hh + 1], OP.mult)
                        mm(dps[:, :], cSLOW[:, :], rhsD[:, :])
                        act(expD[:, :], dps[:, :], AF.Exp)
                        mm(ccps[:, :], Cfm[:, :], Stbf[:, h0 * 64:(h0 + 4) * 64])
                    for hh in range(4):
                        h = h0 + hh
                        if not prev:
                            stt(Mh[:, :], Gm[:, :], dt_tm[:, h:h + 1],
                                expD[:, hh * 128:(hh + 1) * 128], OP.mult, OP.mult)
                            mm(yps[(hh % 2) * 64:(hh % 2) * 64 + 64, :],
                               xtm[:, h * 64:(h + 1) * 64], Mh[:, :],
                               start=True, stop=False)
                            ts(tmpb[:, hh * 64:(hh + 1) * 64],
                               ccps[:, hh * 64:(hh + 1) * 64],
                               eltm[:, h:h + 1], OP.mult)
                            mm(yps[(hh % 2) * 64:(hh % 2) * 64 + 64, :],
                               tmpb[:, hh * 64:(hh + 1) * 64], cIDb[:, :],
                               start=False, stop=True)
                        ts(Vb[:, hh * 64:(hh + 1) * 64], xtm[:, h * 64:(h + 1) * 64],
                           w2tm[:, h:h + 1], OP.mult)
                        if (not prev) and hh % 2 == 1:
                            ftile = h // 2
                            stt(Ysl(ftile, t0, t0 + 128),
                                xc[:, ftile * 128:(ftile + 1) * 128],
                                cDX[:, ftile:ftile + 1], yps[:, :], OP.mult, OP.add)
                    mm(stps[:, :], Btm[:, :], Vb[:, :])
                    for hh in range(4):
                        h = h0 + hh
                        stt(Strun[:, h * 64:(h + 1) * 64],
                            Strun[:, h * 64:(h + 1) * 64],
                            Ebct[:, h:h + 1], stps[:, hh * 64:(hh + 1) * 64],
                            OP.mult, OP.add)

    esB.close()

    with nc.sbuf_tensor("zb", [128, T], BF16) as zb, \
         nc.sbuf_tensor("sqt", [128, 512], F32) as sqt, \
         nc.sbuf_tensor("rrow", [1, T], F32) as rrow, \
         nc.psum_tensor("sq0", [1, 512], F32) as sq0, \
         nc.psum_tensor("sq1", [1, 512], F32) as sq1, \
         nc.psum_tensor("sq2", [1, 512], F32) as sq2, \
         nc.psum_tensor("sq3", [1, 512], F32) as sq3, \
         nc.psum_tensor("rbc", [128, 512], F32) as rbc:
        sqp = [sq0, sq1, sq2, sq3]
        for ft in range(32):
            dma_in(zb[:, :], zsil[ft * 128:(ft + 1) * 128, :])
            ysl = Ysl(ft, 0, T)
            tt(ysl, ysl, zb[:, :], OP.mult)
            for tc in range(4):
                act(sqt[:, :], Ysl(ft, tc * 512, (tc + 1) * 512), AF.Square)
                mm(sqp[tc][:, :], cONES[:, 0:1], sqt[:, :],
                   start=(ft == 0), stop=(ft == 31))
        for tc in range(4):
            r = rrow[:, tc * 512:(tc + 1) * 512]
            ts(r, sqp[tc][:, :], 1.0 / DS, OP.mult, 1e-5, OP.add)
            act(r, r, AF.Sqrt)
            vrecip(r, r)
        for tc in range(4):
            mm(rbc[:, :], cONES[0:1, :], rrow[:, tc * 512:(tc + 1) * 512])
            for ft in range(32):
                ysl = Ysl(ft, tc * 512, (tc + 1) * 512)
                stt(ysl, ysl, cNW[:, ft:ft + 1], rbc[:, :], OP.mult, OP.mult)

    with nc.sbuf_tensor("wob", [128, 32 * 128], BF16) as wob, \
         nc.sbuf_tensor("stg", [128, 16 * 512], F32) as stg, \
         nc.sbuf_tensor("xres", [128, 512], F32) as xres, \
         nc.sbuf_tensor("sqd", [128, 512], F32) as sqd, \
         nc.sbuf_tensor("r1", [1, 512], F32) as r1, \
         nc.sbuf_tensor("r2", [1, 512], F32) as r2, \
         nc.psum_tensor("ops_", [128, 512], F32) as ops_, \
         nc.psum_tensor("s1p", [1, 512], F32) as s1p, \
         nc.psum_tensor("s2p", [1, 512], F32) as s2p, \
         nc.psum_tensor("abc", [128, 512], F32) as abc, \
         nc.psum_tensor("bbc", [128, 512], F32) as bbc:
        for tc in range(4):
            t0 = tc * 512
            for dtile in range(16):
                dma_in(wob[:, :].rearrange("p (kt c) -> p kt c", kt=32),
                       Woutb[:, dtile * 128:(dtile + 1) * 128]
                       .rearrange("(kt p) c -> p kt c", p=128))
                for kt in range(32):
                    mm(ops_[:, :], wob[:, kt * 128:(kt + 1) * 128],
                       Ysl(kt, t0, t0 + 512),
                       start=(kt == 0), stop=(kt == 31))
                dma_in(xres[:, :], x_fm[dtile * 128:(dtile + 1) * 128,
                                        128 + t0:128 + t0 + 512])
                tt(stg[:, dtile * 512:(dtile + 1) * 512], ops_[:, :], xres[:, :],
                   OP.add)
                ssl = stg[:, dtile * 512:(dtile + 1) * 512]
                mm(s1p[:, :], cONES[:, 0:1], ssl,
                   start=(dtile == 0), stop=(dtile == 15))
                act(sqd[:, :], ssl, AF.Square)
                mm(s2p[:, :], cONES[:, 0:1], sqd[:, :],
                   start=(dtile == 0), stop=(dtile == 15))
            ts(r1[:, :], s1p[:, :], 1.0 / HID, OP.mult)
            act(r2[:, :], r1[:, :], AF.Square)
            ts(r2[:, :], r2[:, :], -1.0, OP.mult)
            stt(r2[:, :], s2p[:, :], 1.0 / HID, r2[:, :], OP.mult, OP.add)
            ts(r2[:, :], r2[:, :], 1e-5, OP.add)
            act(r2[:, :], r2[:, :], AF.Sqrt)
            vrecip(r2[:, :], r2[:, :])
            tt(r1[:, :], r1[:, :], r2[:, :], OP.mult)
            ts(r1[:, :], r1[:, :], -1.0, OP.mult)
            mm(abc[:, :], cONES[0:1, :], r2[:, :])
            mm(bbc[:, :], cONES[0:1, :], r1[:, :])
            for dtile in range(16):
                ssl = stg[:, dtile * 512:(dtile + 1) * 512]
                tt(ssl, ssl, abc[:, :], OP.mult)
                tt(Y0[:, dtile * T + t0: dtile * T + t0 + 512], ssl, bbc[:, :],
                   OP.add)

    es2.close()

    with nc.sbuf_tensor("w13f", [128, 4096], F32) as w13f, \
         nc.sbuf_tensor("w13b", [128, 2 * 4096], BF16) as w13b, \
         nc.sbuf_tensor("silb", [128, 512], F32) as silb, \
         nc.sbuf_tensor("hidb", [128, 2 * T], BF16) as hidb, \
         nc.sbuf_tensor("fhb", [128, T], BF16) as fhb, \
         nc.psum_tensor("h1p", [128, 512], F32) as h1p, \
         nc.psum_tensor("h3p", [128, 512], F32) as h3p, \
         nc.psum_tensor("fhp", [128, 512], F32) as fhp:
        for hc in range(32):
            c0 = hc * 256
            dma_in(w13f[:, 0:4096].rearrange("p (kt c) -> p kt c", kt=16),
                   W1T[:, c0:c0 + 256].rearrange("(kt p) c -> p kt c", p=128))
            act(w13b[:, 0:4096], w13f[:, 0:4096], AF.Copy)
            dma_in(w13f[:, 0:4096].rearrange("p (kt c) -> p kt c", kt=16),
                   W3T[:, c0:c0 + 256].rearrange("(kt p) c -> p kt c", p=128))
            act(w13b[:, 4096:8192], w13f[:, 0:4096], AF.Copy)
            for hft in range(2):
                for tc in range(4):
                    t0 = tc * 512
                    for kt in range(16):
                        mm(h1p[:, :], w13b[:, kt * 256 + hft * 128: kt * 256 + hft * 128 + 128],
                           Y0[:, kt * T + t0: kt * T + t0 + 512],
                           start=(kt == 0), stop=(kt == 15))
                    for kt in range(16):
                        mm(h3p[:, :], w13b[:, 4096 + kt * 256 + hft * 128: 4096 + kt * 256 + hft * 128 + 128],
                           Y0[:, kt * T + t0: kt * T + t0 + 512],
                           start=(kt == 0), stop=(kt == 15))
                    act(silb[:, :], h1p[:, :], AF.Silu)
                    tt(hidb[:, hft * T + t0: hft * T + t0 + 512], silb[:, :],
                       h3p[:, :], OP.mult)
                for tc in range(4):
                    t0 = tc * 512
                    mm(fhp[:, :], cH128[:, :], hidb[:, hft * T + t0: hft * T + t0 + 512])
                    act(fhb[:, t0:t0 + 512], fhp[:, :], AF.Copy, scale=RSQ)
                dma_out(fh_d[hc * 2 + hft, :, :], fhb[:, :])

    with nc.sbuf_tensor("fhs", [128, 64 * 512], BF16) as fhs, \
         nc.sbuf_tensor("w2b", [128, 64 * 128], BF16) as w2b, \
         nc.sbuf_tensor("stg2", [128, 16 * 512], BF16) as stg2, \
         nc.sbuf_tensor("x1s", [128, 512], BF16) as x1s, \
         nc.sbuf_tensor("sq2b", [128, 512], F32) as sq2b, \
         nc.sbuf_tensor("ro1", [1, 512], F32) as ro1, \
         nc.sbuf_tensor("ro2", [1, 512], F32) as ro2, \
         nc.sbuf_tensor("oux", [128, 512], F32) as oux, \
         nc.psum_tensor("fps", [128, 512], F32) as fps, \
         nc.psum_tensor("t1p", [1, 512], F32) as t1p, \
         nc.psum_tensor("t2p", [1, 512], F32) as t2p, \
         nc.psum_tensor("ab2", [128, 512], F32) as ab2, \
         nc.psum_tensor("bb2", [128, 512], F32) as bb2:
        for tc in range(4):
            t0 = tc * 512
            for kt in range(64):
                dma_in(fhs[:, kt * 512:(kt + 1) * 512], fh_d[kt, :, t0:t0 + 512])
            for dtile in range(16):
                dma_in(w2b[:, :].rearrange("p (kt c) -> p kt c", kt=64),
                       W2pp[:, dtile * 128:(dtile + 1) * 128]
                       .rearrange("(kt p) c -> p kt c", p=128))
                for kt in range(64):
                    mm(fps[:, :], w2b[:, kt * 128:(kt + 1) * 128],
                       fhs[:, kt * 512:(kt + 1) * 512],
                       start=(kt == 0), stop=(kt == 63))
                ssl = stg2[:, dtile * 512:(dtile + 1) * 512]
                tt(ssl, fps[:, :], Y0[:, dtile * T + t0: dtile * T + t0 + 512],
                   OP.add)
                mm(t1p[:, :], cONESb[:, 0:1], ssl,
                   start=(dtile == 0), stop=(dtile == 15))
                act(sq2b[:, :], ssl, AF.Square)
                mm(t2p[:, :], cONES[:, 0:1], sq2b[:, :],
                   start=(dtile == 0), stop=(dtile == 15))
            ts(ro1[:, :], t1p[:, :], 1.0 / HID, OP.mult)
            act(ro2[:, :], ro1[:, :], AF.Square)
            ts(ro2[:, :], ro2[:, :], -1.0, OP.mult)
            stt(ro2[:, :], t2p[:, :], 1.0 / HID, ro2[:, :], OP.mult, OP.add)
            ts(ro2[:, :], ro2[:, :], 1e-5, OP.add)
            act(ro2[:, :], ro2[:, :], AF.Sqrt)
            vrecip(ro2[:, :], ro2[:, :])
            tt(ro1[:, :], ro1[:, :], ro2[:, :], OP.mult)
            ts(ro1[:, :], ro1[:, :], -1.0, OP.mult)
            mm(ab2[:, :], cONES[0:1, :], ro2[:, :])
            mm(bb2[:, :], cONES[0:1, :], ro1[:, :])
            for dtile in range(16):
                ssl = stg2[:, dtile * 512:(dtile + 1) * 512]
                tt(oux[:, :], ssl, ab2[:, :], OP.mult)
                tt(oux[:, :], oux[:, :], bb2[:, :], OP.add)
                dma_out(out_d[dtile * 128:(dtile + 1) * 128, t0:t0 + 512],
                        oux[:, :])

    ops = E.ops
    totals = dict(E.counts)
    with nc.Block() as block:
        def section(eng_name):
            def body(eng):
                cnt = {e: 0 for e in ENG}
                for (engine, fn, inc, prev) in ops:
                    if engine == eng_name:
                        if prev is not None:
                            pe_, pc_ = prev
                            if not (pe_ == "tensor" and engine == "tensor"):
                                eng.wait_ge(sem_of[pe_], pc_)
                        instr = fn(eng)
                        instr.then_inc(sem_of[engine], inc)
                    cnt[engine] += inc
                if eng_name == "gpsimd":
                    eng.wait_ge(sem_of["gpsimd"], totals["gpsimd"])
            return body
        block.sync(section("sync"))
        block.tensor(section("tensor"))
        block.vector(section("vector"))
        block.scalar(section("scalar"))
        block.gpsimd(section("gpsimd"))

    es.close()
    return nc


def _consts():
    H = np.array([[1.0]], dtype=np.float32)
    while H.shape[0] < 128:
        H = np.block([[H, H], [H, -H]])
    H64 = H[:64, :64].copy()
    HBDm = np.zeros((128, 128), np.float32)
    HBDm[:64, :64] = H64
    HBDm[64:, 64:] = H64
    r = np.arange(128)
    SL = (r[:, None] > r[None, :]).astype(np.float32)
    UT = (r[:, None] <= r[None, :]).astype(np.float32)
    MS = (r[None, :] >= r[:, None]).astype(np.float32)
    IDN = np.eye(128, dtype=np.float32)
    E127v = np.zeros((128, 1), np.float32)
    E127v[127, 0] = 1.0
    return H, H64, HBDm, SL, UT, MS, IDN, E127v


def kernel(L0_attn_hidden_states, L0_attn_residual, L0_attn_W_in, L0_attn_conv_w,
           L0_attn_conv_b, L0_attn_A_log, L0_attn_D, L0_attn_dt_bias,
           L0_attn_norm_w, L0_attn_W_out, L0_attn_mup_vector,
           L0_ffn_W1, L0_ffn_W2, L0_ffn_W3):
    bf = ml_dtypes.bfloat16
    x = np.asarray(L0_attn_hidden_states, np.float32)
    B, L, _ = x.shape
    H, H64, HBDm, SL, UT, MS, IDN, E127v = _consts()

    W2T = np.ascontiguousarray(np.asarray(L0_ffn_W2, np.float32).T)
    W2q = np.ascontiguousarray(
        W2T.reshape(64, 2, 64, HID).transpose(2, 1, 0, 3)).reshape(FFH, HID)

    shared = {
        "Win": np.ascontiguousarray(np.asarray(L0_attn_W_in, np.float32)),
        "Wout": np.ascontiguousarray(np.asarray(L0_attn_W_out, np.float32)),
        "W1T": np.ascontiguousarray(np.asarray(L0_ffn_W1, np.float32).T),
        "W3T": np.ascontiguousarray(np.asarray(L0_ffn_W3, np.float32).T),
        "W2q": W2q,
        "convw": np.ascontiguousarray(np.asarray(L0_attn_conv_w, np.float32)),
        "convb": np.asarray(L0_attn_conv_b, np.float32).reshape(CONV, 1),
        "mupc": np.concatenate([np.asarray(L0_attn_mup_vector, np.float32),
            np.zeros(67 * 128 - PROJ, np.float32)]).reshape(67 * 128, 1),
        "normw": np.asarray(L0_attn_norm_w, np.float32).reshape(DS, 1),
        "Dexp": np.repeat(np.asarray(L0_attn_D, np.float32), P).reshape(DS, 1),
        "Acol": np.asarray(L0_attn_A_log, np.float32).reshape(NH, 1),
        "Arow": np.asarray(L0_attn_A_log, np.float32).reshape(1, NH),
        "dtbc": np.asarray(L0_attn_dt_bias, np.float32).reshape(NH, 1),
        "ONES": np.ones((128, 128), np.float32),
        "ONESb": np.ones((128, 128), bf),
        "E127": E127v,
        "SLOW": SL, "UTRI": UT,
        "MSK01": MS.astype(bf),
        "IDNb": IDN.astype(bf), "IDNf": IDN,
        "H128": H.astype(bf), "HBD": HBDm.astype(bf),
    }

    in_maps = []
    for c in range(8):
        b, h = c // 2, c % 2
        xT = np.ascontiguousarray(x[b].T)
        own = xT[:, h * T:(h + 1) * T]
        ext = np.zeros((HID, TEXT), np.float32)
        ext[:, 128:] = own
        if h == 1:
            ext[:, 125:128] = xT[:, T - 3:T]
        m = dict(shared)
        m["x_fm"] = ext
        if h == 1:
            pext = np.zeros((HID, TEXT), np.float32)
            pext[:, 128:] = xT[:, 0:T]
            m["x_pfm"] = pext
        else:
            m["x_pfm"] = np.zeros((HID, TEXT), np.float32)
        m["maskc"] = np.full((128, 1), float(h), np.float32)
        in_maps.append(m)

    nc = build_nc()
    res = run_bass_kernel_spmd(nc, in_maps, core_ids=list(range(8)))
    out = np.empty((B, L, HID), np.float32)
    for c in range(8):
        b, h = c // 2, c % 2
        out[b, h * T:(h + 1) * T, :] = res.results[c]["out_d"].T
    return out


# version 05 (final)
# speedup vs baseline: 1.5251x; 1.0337x over previous
import numpy as np
import ml_dtypes
import concourse.bass as bass
import concourse.mybir as mybir
from concourse.bass_utils import run_bass_kernel_spmd
from contextlib import ExitStack

F32 = mybir.dt.float32
BF16 = mybir.dt.bfloat16
AF = mybir.ActivationFunctionType
OP = mybir.AluOpType

HID = 2048
DS = 4096
S = 128
NH = 64
P = 64
KCONV = 4
CONV = DS + 2 * S
PROJ = 2 * DS + 2 * S + NH
T = 2048
TEXT = T + 128
NCH = T // 128
FFH = 8192
RSQ = 1.0 / np.sqrt(FFH)

ENG = ("sync", "tensor", "vector", "scalar", "gpsimd")


class Em:

    def __init__(self):
        self.ops = []
        self.counts = {e: 0 for e in ENG}
        self.last = None

    def emit(self, engine, fn, inc, deps=None):
        if deps is None:
            deps = [self.last] if self.last is not None else []
        else:
            deps = [d for d in deps if d is not None]
        self.counts[engine] += inc
        self.ops.append((engine, fn, inc, deps))
        self.last = (engine, self.counts[engine])
        return self.last


E = None


def dma_in(dst, src, slow=False, deps=None):
    return E.emit("sync", lambda e, d=dst, s=src, sl=slow: e.dma_start(
        out=d, in_=s, allow_slow_non_contiguous=sl) if sl else e.dma_start(out=d, in_=s), 16, deps)


def dma_out(dst, src, slow=False):
    E.emit("gpsimd", lambda e, d=dst, s=src, sl=slow: e.dma_start(
        out=d, in_=s, allow_slow_non_contiguous=sl) if sl else e.dma_start(out=d, in_=s), 16)


def mm(out, lhsT, rhs, start=True, stop=True, deps=None):
    return E.emit("tensor", lambda e, o=out, l=lhsT, r=rhs, st=start, sp=stop:
           e.matmul(o, l, r, start=st, stop=sp), 1, deps)


def petr(out, in_, idn):
    E.emit("tensor", lambda e, o=out, i=in_, d=idn: e.transpose(o, i, d), 1)


def act(out, in_, func, bias=0.0, scale=1.0, deps=None):
    return E.emit("scalar", lambda e, o=out, i=in_, f=func, b=bias, s=scale:
           e.activation(o, i, f, bias=b, scale=s), 1, deps)


def tt(out, in0, in1, op, deps=None):
    return E.emit("vector", lambda e, o=out, a=in0, b=in1, p=op: e.tensor_tensor(o, a, b, p), 1, deps)


def ts(out, in0, s1, op0, s2=None, op1=OP.bypass):
    E.emit("vector", lambda e, o=out, a=in0, x=s1, y=s2, p0=op0, p1=op1:
           e.tensor_scalar(o, a, x, y, p0, p1), 1)


def stt(out, in0, scalar, in1, op0, op1):
    E.emit("vector", lambda e, o=out, a=in0, s=scalar, b=in1, p0=op0, p1=op1:
           e.scalar_tensor_tensor(o, a, s, b, p0, p1), 1)


def ttscan(out, d0, d1, init, op0, op1):
    E.emit("vector", lambda e, o=out, a=d0, b=d1, i=init, p0=op0, p1=op1:
           e.tensor_tensor_scan(o, a, b, i, p0, p1), 1)


def vrecip(out, in_):
    E.emit("vector", lambda e, o=out, i=in_: e.reciprocal(o, i), 1)


def memset(ap, v):
    E.emit("vector", lambda e, a=ap, c=v: e.memset(a, c), 1)


def build_nc():
    global E
    E = Em()
    nc = bass.Bass("TRN2", target_bir_lowering=False, debug=False)

    x_fm = nc.dram_tensor("x_fm", [HID, TEXT], F32, kind="ExternalInput").ap()
    x_pfm = nc.dram_tensor("x_pfm", [HID, TEXT], F32, kind="ExternalInput").ap()
    maskc = nc.dram_tensor("maskc", [128, 1], F32, kind="ExternalInput").ap()
    Win = nc.dram_tensor("Win", [HID, PROJ], F32, kind="ExternalInput").ap()
    Wout = nc.dram_tensor("Wout", [DS, HID], F32, kind="ExternalInput").ap()
    W1T = nc.dram_tensor("W1T", [HID, FFH], F32, kind="ExternalInput").ap()
    W3T = nc.dram_tensor("W3T", [HID, FFH], F32, kind="ExternalInput").ap()
    W2q = nc.dram_tensor("W2q", [FFH, HID], F32, kind="ExternalInput").ap()
    convw = nc.dram_tensor("convw", [CONV, KCONV], F32, kind="ExternalInput").ap()
    convb = nc.dram_tensor("convb", [CONV, 1], F32, kind="ExternalInput").ap()
    mupc = nc.dram_tensor("mupc", [67 * 128, 1], F32, kind="ExternalInput").ap()
    normw = nc.dram_tensor("normw", [DS, 1], F32, kind="ExternalInput").ap()
    Dexp = nc.dram_tensor("Dexp", [DS, 1], F32, kind="ExternalInput").ap()
    Acol = nc.dram_tensor("Acol", [NH, 1], F32, kind="ExternalInput").ap()
    dtbc = nc.dram_tensor("dtbc", [NH, 1], F32, kind="ExternalInput").ap()
    Arow = nc.dram_tensor("Arow", [1, NH], F32, kind="ExternalInput").ap()
    ONES = nc.dram_tensor("ONES", [128, 128], F32, kind="ExternalInput").ap()
    ONESb = nc.dram_tensor("ONESb", [128, 128], BF16, kind="ExternalInput").ap()
    E127 = nc.dram_tensor("E127", [128, 1], F32, kind="ExternalInput").ap()
    SLOW = nc.dram_tensor("SLOW", [128, 128], F32, kind="ExternalInput").ap()
    UTRI = nc.dram_tensor("UTRI", [128, 128], F32, kind="ExternalInput").ap()
    MSK01 = nc.dram_tensor("MSK01", [128, 128], BF16, kind="ExternalInput").ap()
    IDNb = nc.dram_tensor("IDNb", [128, 128], BF16, kind="ExternalInput").ap()
    IDNf = nc.dram_tensor("IDNf", [128, 128], F32, kind="ExternalInput").ap()
    H128 = nc.dram_tensor("H128", [128, 128], BF16, kind="ExternalInput").ap()
    HBD = nc.dram_tensor("HBD", [128, 128], BF16, kind="ExternalInput").ap()
    out_d = nc.dram_tensor("out_d", [HID, T], F32, kind="ExternalOutput").ap()

    xBCs = nc.dram_tensor("xBCs", [CONV, T], BF16).ap()
    zsil = nc.dram_tensor("zsil", [DS, T], BF16).ap()
    dt_d = nc.dram_tensor("dt_d", [NH, T], F32).ap()
    Woutb = nc.dram_tensor("Woutb", [DS, HID], BF16).ap()
    W2pp = nc.dram_tensor("W2pp", [FFH, HID], BF16).ap()
    fh_d = nc.dram_tensor("fh_d", [FFH // 128, 128, T], BF16).ap()
    xBCp = nc.dram_tensor("xBCp", [CONV, T], BF16).ap()
    dtp = nc.dram_tensor("dtp", [NH, T], F32).ap()

    es = ExitStack()
    sem = {}
    for name in ("s_ld", "s_st", "s_pe", "s_act", "s_dve"):
        sem[name] = es.enter_context(nc.semaphore(name))
    sem_of = {"sync": sem["s_ld"], "gpsimd": sem["s_st"], "tensor": sem["s_pe"],
              "scalar": sem["s_act"], "vector": sem["s_dve"]}

    def sb(name, p, f, dt):
        return es.enter_context(nc.sbuf_tensor(name, [p, f], dt))

    def ps(name, p, f, dt=F32):
        return es.enter_context(nc.psum_tensor(name, [p, f], dt))

    cArow = sb("cArow", 1, NH, F32)
    cONESb = sb("cONESb", 128, 128, BF16)
    cONES = sb("cONES", 128, 128, F32)
    cE127 = sb("cE127", 128, 1, F32)
    cSLOW = sb("cSLOW", 128, 128, F32)
    cUTRI = sb("cUTRI", 128, 128, F32)
    cMSK = sb("cMSK", 128, 128, BF16)
    cIDb = sb("cIDb", 128, 128, BF16)
    cIDf = sb("cIDf", 128, 128, F32)
    cH128 = sb("cH128", 128, 128, BF16)
    cHBD = sb("cHBD", 128, 128, BF16)
    cMUP = sb("cMUP", 128, 67, F32)
    cCW = sb("cCW", 128, 34 * KCONV, F32)
    cCB = sb("cCB", 128, 34, F32)
    cNW = sb("cNW", 128, 32, F32)
    cDX = sb("cDX", 128, 32, F32)
    cA = sb("cA", 64, 1, F32)
    cDTB = sb("cDTB", 64, 1, F32)
    cNEA = sb("cNEA", 64, 1, F32)
    cMASKC = sb("cMASKC", 128, 1, F32)

    for d, s in ((cONES, ONES), (cONESb, ONESb), (cArow, Arow),
                 (cE127, E127), (cSLOW, SLOW), (cUTRI, UTRI),
                 (cMSK, MSK01), (cIDb, IDNb), (cIDf, IDNf), (cH128, H128),
                 (cHBD, HBD), (cA, Acol), (cDTB, dtbc), (cMASKC, maskc)):
        dma_in(d[:, :], s[:, :])
    dma_in(cMUP[:, :].rearrange("p (ft c) -> p ft c", c=1), mupc.rearrange("(ft p) c -> p ft c", p=128), slow=True)
    dma_in(cCW[:, :].rearrange("p (ft k) -> p ft k", k=4), convw.rearrange("(ft p) k -> p ft k", p=128), slow=True)
    dma_in(cCB[:, :].rearrange("p (ft c) -> p ft c", c=1), convb.rearrange("(ft p) c -> p ft c", p=128), slow=True)
    dma_in(cNW[:, :].rearrange("p (ft c) -> p ft c", c=1), normw.rearrange("(ft p) c -> p ft c", p=128), slow=True)
    dma_in(cDX[:, :].rearrange("p (ft c) -> p ft c", c=1), Dexp.rearrange("(ft p) c -> p ft c", p=128), slow=True)
    act(cNEA[:, :], cA[:, :], AF.Exp)
    ts(cNEA[:, :], cNEA[:, :], -1.0, OP.mult)

    with nc.sbuf_tensor("p0a", [128, HID], F32) as p0a, \
         nc.sbuf_tensor("p0b", [128, HID], BF16) as p0b, \
         nc.sbuf_tensor("p0o", [128, HID], BF16) as p0o, \
         nc.psum_tensor("p0ps", [128, 512], F32) as p0ps:
        W2v = W2pp.rearrange("(a g b) d -> g a (b d)", g=2, b=64)
        for b0 in range(64):
            dma_in(p0a[:, :], W2q[b0 * 128:(b0 + 1) * 128, :])
            act(p0b[:, :], p0a[:, :], AF.Copy)
            for c in range(4):
                mm(p0ps[:, :], cHBD[:, :], p0b[:, c * 512:(c + 1) * 512])
                act(p0o[:, c * 512:(c + 1) * 512], p0ps[:, :], AF.Copy)
            dma_out(W2v[:, :, b0 * HID:(b0 + 1) * HID], p0o[:, :])

    with nc.sbuf_tensor("xbf", [128, 16 * TEXT], BF16) as xbf, \
         nc.sbuf_tensor("xbf2", [128, 16 * TEXT], BF16) as xbf2, \
         nc.sbuf_tensor("wf32", [128, HID], F32) as wf32, \
         nc.sbuf_tensor("wbf", [128, HID], BF16) as wbf, \
         nc.sbuf_tensor("pbuf", [128, TEXT], F32) as pbuf, \
         nc.sbuf_tensor("obuf", [128, T], BF16) as obuf, \
         nc.sbuf_tensor("aux", [128, TEXT], F32) as aux, \
         nc.psum_tensor("aps", [128, 512], F32) as aps:
        for kt in range(16):
            dma_in(aux[:, 0:TEXT], x_fm[kt * 128:(kt + 1) * 128, :])
            act(xbf[:, kt * TEXT:(kt + 1) * TEXT], aux[:, 0:TEXT], AF.Copy)
            dma_in(aux[:, 0:TEXT], x_pfm[kt * 128:(kt + 1) * 128, :])
            act(xbf2[:, kt * TEXT:(kt + 1) * TEXT], aux[:, 0:TEXT], AF.Copy)

        def load_win(c0, w):
            dma_in(wf32[:, 0:16 * w].rearrange("p (kt c) -> p kt c", kt=16),
                   Win[:, c0:c0 + w].rearrange("(kt p) c -> p kt c", p=128))
            act(wbf[:, 0:16 * w], wf32[:, 0:16 * w], AF.Copy)

        for ft in range(34):
            c0 = DS + ft * 128
            gmt = c0 // 128
            load_win(c0, 128)
            for xsrc, dst, masked in ((xbf2, xBCp, True), (xbf, xBCs, False)):
                for tc in range(5):
                    t0 = tc * 512
                    w = 512 if tc < 4 else TEXT - 4 * 512
                    for kt in range(16):
                        mm(aps[:, 0:w], wbf[:, kt * 128:(kt + 1) * 128],
                           xsrc[:, kt * TEXT + t0: kt * TEXT + t0 + w],
                           start=(kt == 0), stop=(kt == 15))
                    ts(pbuf[:, t0:t0 + w], aps[:, 0:w], cMUP[:, gmt:gmt + 1], OP.mult)
                ts(aux[:, 0:T], pbuf[:, 125:125 + T], cCW[:, ft * 4:ft * 4 + 1],
                   OP.mult, cCB[:, ft:ft + 1], OP.add)
                for k in range(1, 4):
                    stt(aux[:, 0:T], pbuf[:, 125 + k:125 + k + T],
                        cCW[:, ft * 4 + k:ft * 4 + k + 1], aux[:, 0:T], OP.mult, OP.add)
                act(obuf[:, :], aux[:, 0:T], AF.Silu)
                if masked:
                    ts(obuf[:, :], obuf[:, :], cMASKC[:, :], OP.mult)
                dma_out(dst[ft * 128:(ft + 1) * 128, :], obuf[:, :])
        dma_in(wf32[:, 0:16 * 64].rearrange("p (kt c) -> p kt c", kt=16),
               Win[:, PROJ - 64:PROJ].rearrange("(kt p) c -> p kt c", p=128))
        act(wbf[:, 0:16 * 64], wf32[:, 0:16 * 64], AF.Copy)
        for xsrc, dst in ((xbf2, dtp), (xbf, dt_d)):
            for tc in range(4):
                t0 = 128 + tc * 512
                for kt in range(16):
                    mm(aps[0:64, 0:512], wbf[:, kt * 64:(kt + 1) * 64],
                       xsrc[:, kt * TEXT + t0: kt * TEXT + t0 + 512],
                       start=(kt == 0), stop=(kt == 15))
                ts(aux[0:64, 0:512], aps[0:64, 0:512], cMUP[0:64, 66:67], OP.mult)
                dma_out(dst[:, tc * 512:(tc + 1) * 512], aux[0:64, 0:512])
        for ft in range(32):
            c0 = ft * 128
            load_win(c0, 128)
            for tc in range(4):
                t0 = 128 + tc * 512
                for kt in range(16):
                    mm(aps[:, 0:512], wbf[:, kt * 128:(kt + 1) * 128],
                       xbf[:, kt * TEXT + t0: kt * TEXT + t0 + 512],
                       start=(kt == 0), stop=(kt == 15))
                ts(aux[:, tc * 512:(tc + 1) * 512], aps[:, 0:512],
                   cMUP[:, ft:ft + 1], OP.mult)
            act(obuf[:, :], aux[:, 0:T], AF.Silu)
            dma_out(zsil[ft * 128:(ft + 1) * 128, :], obuf[:, :])
        for kt in range(32):
            dma_in(aux[:, 0:HID], Wout[kt * 128:(kt + 1) * 128, :])
            act(obuf[:, 0:HID], aux[:, 0:HID], AF.Copy)
            dma_out(Woutb[kt * 128:(kt + 1) * 128, :], obuf[:, 0:HID])

    Y0 = sb("Y0", 128, 16 * T, BF16)
    es2 = ExitStack()
    Y1 = es2.enter_context(nc.sbuf_tensor("Y1", [128, 16 * T], BF16))

    def Ysl(ft, a, b):
        buf = Y0 if ft < 16 else Y1
        return buf[:, (ft % 16) * T + a: (ft % 16) * T + b]
    esB = ExitStack()
    xc = esB.enter_context(nc.sbuf_tensor("xc", [128, 32 * 128], BF16))
    xtm = esB.enter_context(nc.sbuf_tensor("xtm", [128, 32 * 128], BF16))
    Bfm = esB.enter_context(nc.sbuf_tensor("Bfm", [128, 128], BF16))
    Cfm = esB.enter_context(nc.sbuf_tensor("Cfm", [128, 128], BF16))
    Btm = esB.enter_context(nc.sbuf_tensor("Btm", [128, 128], BF16))
    dtr = esB.enter_context(nc.sbuf_tensor("dtr", [64, 512], F32))
    Strun = esB.enter_context(nc.sbuf_tensor("Strun", [128, NH * P], F32))
    Stbf = esB.enter_context(nc.sbuf_tensor("Stbf", [128, NH * P], BF16))
    Tst = esB.enter_context(nc.sbuf_tensor("Tst", [64, 256], F32))
    Tst2 = esB.enter_context(nc.sbuf_tensor("Tst2", [64, 256], F32))
    TM = esB.enter_context(nc.sbuf_tensor("TM", [128, 128], F32))
    ldatm = esB.enter_context(nc.sbuf_tensor("ldatm", [128, 64], F32))
    neabc = esB.enter_context(nc.sbuf_tensor("neabc", [128, 64], F32))
    eltm = esB.enter_context(nc.sbuf_tensor("eltm", [128, 64], F32))
    w2tm = esB.enter_context(nc.sbuf_tensor("w2tm", [128, 64], F32))
    Ebct = esB.enter_context(nc.sbuf_tensor("Ebct", [128, 64], F32))
    rtot = esB.enter_context(nc.sbuf_tensor("rtot", [1, 64], F32))
    Gm = esB.enter_context(nc.sbuf_tensor("Gm", [128, 128], BF16))
    rhsD = esB.enter_context(nc.sbuf_tensor("rhsD", [128, 512], F32))
    expD = esB.enter_context(nc.sbuf_tensor("expD", [128, 512], BF16))
    Mh = esB.enter_context(nc.sbuf_tensor("Mh", [128, 128], BF16))
    Vb = esB.enter_context(nc.sbuf_tensor("Vb", [128, 256], BF16))
    tmpb = esB.enter_context(nc.sbuf_tensor("tmpb", [128, 256], BF16))
    sc1 = esB.enter_context(nc.sbuf_tensor("sc1", [128, 64], F32))
    trp = esB.enter_context(nc.psum_tensor("trp", [128, 512], BF16))
    gps = esB.enter_context(nc.psum_tensor("gps", [128, 128], F32))
    dps = esB.enter_context(nc.psum_tensor("dps", [128, 512], F32))
    yps = esB.enter_context(nc.psum_tensor("yps", [128, 128], F32))
    ccps = esB.enter_context(nc.psum_tensor("ccps", [128, 256], F32))
    stps = esB.enter_context(nc.psum_tensor("stps", [128, 256], F32))
    bps = esB.enter_context(nc.psum_tensor("bps", [128, 128], F32))
    tsp = esB.enter_context(nc.psum_tensor("tsp", [128, 128], F32))
    if True:
        memset(Strun[:, :], 0.0)
        mm(bps[:, 0:64], cONES[0:1, :], cArow[:, :])
        act(neabc[:, :], bps[:, 0:64], AF.Exp)
        ts(neabc[:, :], neabc[:, :], -1.0, OP.mult)
        for prev in (True, False):
            xBCsrc = xBCp if prev else xBCs
            dtsrc = dtp if prev else dt_d
            for j in range(NCH):
                t0 = j * 128
                if j % 4 == 0:
                    dma_in(dtr[:, :], dtsrc[:, t0:t0 + 512])
                dma_in(xc[:, :].rearrange("p (ft t) -> p ft t", ft=32),
                       xBCsrc[0:DS, t0:t0 + 128].rearrange("(ft p) t -> p ft t", p=128))
                dma_in(Bfm[:, :], xBCsrc[DS:DS + 128, t0:t0 + 128])
                if not prev:
                    dma_in(Cfm[:, :], xBCsrc[DS + 128:DS + 256, t0:t0 + 128])
                for ft in range(32):
                    petr(trp[:, (ft % 4) * 128:(ft % 4) * 128 + 128],
                         xc[:, ft * 128:(ft + 1) * 128], cIDb[:, :])
                    if ft % 4 == 3:
                        act(xtm[:, (ft - 3) * 128:(ft + 1) * 128], trp[:, :], AF.Copy)
                petr(trp[:, 0:128], Bfm[:, :], cIDb[:, :])
                act(Btm[:, :], trp[:, 0:128], AF.Copy)
                dc = dtr[:, (t0 % 512):(t0 % 512) + 128]
                u1 = Tst[0:64, 0:128]
                u2 = Tst2[0:64, 0:128]
                spb = Tst2[0:64, 128:256]
                csb = Tst[0:64, 128:256]
                ts(u1, dc, cDTB[:, :], OP.add)
                act(u2, u1, AF.Abs)
                act(u2, u2, AF.Exp, scale=-1.0)
                act(u2, u2, AF.Ln, bias=1.0)
                ts(u1, u1, 0.0, OP.max)
                tt(spb, u1, u2, OP.add)
                ts(u1, spb, cNEA[:, :], OP.mult)
                ttscan(csb, cONES[0:64, :], u1, 0.0, OP.mult, OP.add)
                petr(tsp[:, 0:64], csb, cIDf[0:64, 0:64])
                act(TM[:, 0:64], tsp[:, 0:64], AF.Copy)
                petr(tsp[:, 64:128], spb, cIDf[0:64, 0:64])
                act(TM[:, 64:128], tsp[:, 64:128], AF.Copy)
                cs_tm = TM[:, 0:64]
                dt_tm = TM[:, 64:128]
                mm(bps[0:1, 0:64], cE127[:, :], cs_tm)
                act(rtot[:, :], bps[0:1, 0:64], AF.Copy)
                mm(bps[:, 0:64], cONES[0:1, :], rtot[:, :])
                act(Ebct[:, :], bps[:, 0:64], AF.Exp)
                tt(sc1[:, :], bps[:, 0:64], cs_tm, OP.subtract)
                act(sc1[:, :], sc1[:, :], AF.Exp)
                tt(w2tm[:, :], sc1[:, :], dt_tm, OP.mult)
                if not prev:
                    tt(ldatm[:, :], dt_tm, neabc[:, :], OP.mult)
                    act(eltm[:, :], cs_tm, AF.Exp)
                    mm(gps[:, :], Bfm[:, :], Cfm[:, :])
                    tt(Gm[:, :], gps[:, :], cMSK[:, :], OP.mult)
                    act(Stbf[:, :], Strun[:, :], AF.Copy)
                for g in range(16):
                    h0 = 4 * g
                    if not prev:
                        for hh in range(4):
                            ts(rhsD[:, hh * 128:(hh + 1) * 128], cUTRI[:, :],
                               ldatm[:, h0 + hh:h0 + hh + 1], OP.mult)
                        mm(dps[:, :], cSLOW[:, :], rhsD[:, :])
                        act(expD[:, :], dps[:, :], AF.Exp)
                        mm(ccps[:, :], Cfm[:, :], Stbf[:, h0 * 64:(h0 + 4) * 64])
                    for hh in range(4):
                        h = h0 + hh
                        if not prev:
                            stt(Mh[:, :], Gm[:, :], dt_tm[:, h:h + 1],
                                expD[:, hh * 128:(hh + 1) * 128], OP.mult, OP.mult)
                            mm(yps[(hh % 2) * 64:(hh % 2) * 64 + 64, :],
                               xtm[:, h * 64:(h + 1) * 64], Mh[:, :],
                               start=True, stop=False)
                            ts(tmpb[:, hh * 64:(hh + 1) * 64],
                               ccps[:, hh * 64:(hh + 1) * 64],
                               eltm[:, h:h + 1], OP.mult)
                            mm(yps[(hh % 2) * 64:(hh % 2) * 64 + 64, :],
                               tmpb[:, hh * 64:(hh + 1) * 64], cIDb[:, :],
                               start=False, stop=True)
                        ts(Vb[:, hh * 64:(hh + 1) * 64], xtm[:, h * 64:(h + 1) * 64],
                           w2tm[:, h:h + 1], OP.mult)
                        if (not prev) and hh % 2 == 1:
                            ftile = h // 2
                            stt(Ysl(ftile, t0, t0 + 128),
                                xc[:, ftile * 128:(ftile + 1) * 128],
                                cDX[:, ftile:ftile + 1], yps[:, :], OP.mult, OP.add)
                    mm(stps[:, :], Btm[:, :], Vb[:, :])
                    for hh in range(4):
                        h = h0 + hh
                        stt(Strun[:, h * 64:(h + 1) * 64],
                            Strun[:, h * 64:(h + 1) * 64],
                            Ebct[:, h:h + 1], stps[:, hh * 64:(hh + 1) * 64],
                            OP.mult, OP.add)

    esB.close()

    with nc.sbuf_tensor("zb", [128, T], BF16) as zb, \
         nc.sbuf_tensor("sqt", [128, 512], F32) as sqt, \
         nc.sbuf_tensor("rrow", [1, T], F32) as rrow, \
         nc.psum_tensor("sq0", [1, 512], F32) as sq0, \
         nc.psum_tensor("sq1", [1, 512], F32) as sq1, \
         nc.psum_tensor("sq2", [1, 512], F32) as sq2, \
         nc.psum_tensor("sq3", [1, 512], F32) as sq3, \
         nc.psum_tensor("rbc", [128, 512], F32) as rbc:
        sqp = [sq0, sq1, sq2, sq3]
        for ft in range(32):
            dma_in(zb[:, :], zsil[ft * 128:(ft + 1) * 128, :])
            ysl = Ysl(ft, 0, T)
            tt(ysl, ysl, zb[:, :], OP.mult)
            for tc in range(4):
                act(sqt[:, :], Ysl(ft, tc * 512, (tc + 1) * 512), AF.Square)
                mm(sqp[tc][:, :], cONES[:, 0:1], sqt[:, :],
                   start=(ft == 0), stop=(ft == 31))
        for tc in range(4):
            r = rrow[:, tc * 512:(tc + 1) * 512]
            ts(r, sqp[tc][:, :], 1.0 / DS, OP.mult, 1e-5, OP.add)
            act(r, r, AF.Sqrt)
            vrecip(r, r)
        for tc in range(4):
            mm(rbc[:, :], cONES[0:1, :], rrow[:, tc * 512:(tc + 1) * 512])
            for ft in range(32):
                ysl = Ysl(ft, tc * 512, (tc + 1) * 512)
                stt(ysl, ysl, cNW[:, ft:ft + 1], rbc[:, :], OP.mult, OP.mult)

    with nc.sbuf_tensor("wob", [128, 32 * 128], BF16) as wob, \
         nc.sbuf_tensor("stg", [128, 16 * 512], F32) as stg, \
         nc.sbuf_tensor("xres", [128, 512], F32) as xres, \
         nc.sbuf_tensor("sqd", [128, 512], F32) as sqd, \
         nc.sbuf_tensor("r1", [1, 512], F32) as r1, \
         nc.sbuf_tensor("r2", [1, 512], F32) as r2, \
         nc.psum_tensor("ops_", [128, 512], F32) as ops_, \
         nc.psum_tensor("s1p", [1, 512], F32) as s1p, \
         nc.psum_tensor("s2p", [1, 512], F32) as s2p, \
         nc.psum_tensor("abc", [128, 512], F32) as abc, \
         nc.psum_tensor("bbc", [128, 512], F32) as bbc:
        for tc in range(4):
            t0 = tc * 512
            for dtile in range(16):
                dma_in(wob[:, :].rearrange("p (kt c) -> p kt c", kt=32),
                       Woutb[:, dtile * 128:(dtile + 1) * 128]
                       .rearrange("(kt p) c -> p kt c", p=128))
                for kt in range(32):
                    mm(ops_[:, :], wob[:, kt * 128:(kt + 1) * 128],
                       Ysl(kt, t0, t0 + 512),
                       start=(kt == 0), stop=(kt == 31))
                dma_in(xres[:, :], x_fm[dtile * 128:(dtile + 1) * 128,
                                        128 + t0:128 + t0 + 512])
                tt(stg[:, dtile * 512:(dtile + 1) * 512], ops_[:, :], xres[:, :],
                   OP.add)
                ssl = stg[:, dtile * 512:(dtile + 1) * 512]
                mm(s1p[:, :], cONES[:, 0:1], ssl,
                   start=(dtile == 0), stop=(dtile == 15))
                act(sqd[:, :], ssl, AF.Square)
                mm(s2p[:, :], cONES[:, 0:1], sqd[:, :],
                   start=(dtile == 0), stop=(dtile == 15))
            ts(r1[:, :], s1p[:, :], 1.0 / HID, OP.mult)
            act(r2[:, :], r1[:, :], AF.Square)
            ts(r2[:, :], r2[:, :], -1.0, OP.mult)
            stt(r2[:, :], s2p[:, :], 1.0 / HID, r2[:, :], OP.mult, OP.add)
            ts(r2[:, :], r2[:, :], 1e-5, OP.add)
            act(r2[:, :], r2[:, :], AF.Sqrt)
            vrecip(r2[:, :], r2[:, :])
            tt(r1[:, :], r1[:, :], r2[:, :], OP.mult)
            ts(r1[:, :], r1[:, :], -1.0, OP.mult)
            mm(abc[:, :], cONES[0:1, :], r2[:, :])
            mm(bbc[:, :], cONES[0:1, :], r1[:, :])
            for dtile in range(16):
                ssl = stg[:, dtile * 512:(dtile + 1) * 512]
                tt(ssl, ssl, abc[:, :], OP.mult)
                tt(Y0[:, dtile * T + t0: dtile * T + t0 + 512], ssl, bbc[:, :],
                   OP.add)

    es2.close()

    with nc.sbuf_tensor("w13f", [128, 4096], F32) as w13f, \
         nc.sbuf_tensor("w13b", [128, 2 * 4096], BF16) as w13b, \
         nc.sbuf_tensor("silb", [128, 1024], F32) as silb, \
         nc.sbuf_tensor("hidb", [128, 2 * T], BF16) as hidb, \
         nc.sbuf_tensor("fhb", [128, T], BF16) as fhb, \
         nc.psum_tensor("h1p", [128, 1024], F32) as h1p, \
         nc.psum_tensor("h3p", [128, 1024], F32) as h3p, \
         nc.psum_tensor("fhp", [128, 512], F32) as fhp:
        Etok = {"c3": None, "tt": [None, None], "act": None, "dve": None,
                "last_mm": None}
        for hc in range(32):
            c0 = hc * 256
            d1 = dma_in(w13f[:, 0:4096].rearrange("p (kt c) -> p kt c", kt=16),
                        W1T[:, c0:c0 + 256].rearrange("(kt p) c -> p kt c", p=128))
            c1 = act(w13b[:, 0:4096], w13f[:, 0:4096], AF.Copy,
                     deps=[d1, Etok["last_mm"], Etok["act"]])
            d3 = dma_in(w13f[:, 0:4096].rearrange("p (kt c) -> p kt c", kt=16),
                        W3T[:, c0:c0 + 256].rearrange("(kt p) c -> p kt c", p=128),
                        deps=[c1])
            c3 = act(w13b[:, 4096:8192], w13f[:, 0:4096], AF.Copy, deps=[d3, c1])
            Etok["c3"] = c3
            Etok["act"] = c3
            for hft in range(2):
                for tc in range(4):
                    t0 = tc * 512
                    par = tc % 2
                    gdeps = [Etok["c3"], Etok["tt"][par]]
                    m1 = None
                    for kt in range(16):
                        m1 = mm(h1p[:, par * 512:par * 512 + 512],
                                w13b[:, kt * 256 + hft * 128: kt * 256 + hft * 128 + 128],
                                Y0[:, kt * T + t0: kt * T + t0 + 512],
                                start=(kt == 0), stop=(kt == 15),
                                deps=(gdeps if kt == 0 else []))
                    m3 = None
                    for kt in range(16):
                        m3 = mm(h3p[:, par * 512:par * 512 + 512],
                                w13b[:, 4096 + kt * 256 + hft * 128: 4096 + kt * 256 + hft * 128 + 128],
                                Y0[:, kt * T + t0: kt * T + t0 + 512],
                                start=(kt == 0), stop=(kt == 15), deps=[])
                    sl = act(silb[:, par * 512:par * 512 + 512],
                             h1p[:, par * 512:par * 512 + 512], AF.Silu,
                             deps=[m1, Etok["act"], Etok["tt"][par]])
                    Etok["act"] = sl
                    Etok["tt"][par] = tt(
                        hidb[:, hft * T + t0: hft * T + t0 + 512],
                        silb[:, par * 512:par * 512 + 512],
                        h3p[:, par * 512:par * 512 + 512], OP.mult,
                        deps=[sl, m3, Etok["dve"]])
                    Etok["dve"] = Etok["tt"][par]
                    Etok["last_mm"] = m3
                for tc in range(4):
                    t0 = tc * 512
                    mm(fhp[:, :], cH128[:, :], hidb[:, hft * T + t0: hft * T + t0 + 512])
                    act(fhb[:, t0:t0 + 512], fhp[:, :], AF.Copy, scale=RSQ)
                dma_out(fh_d[hc * 2 + hft, :, :], fhb[:, :])

    with nc.sbuf_tensor("fhs", [128, 64 * 512], BF16) as fhs, \
         nc.sbuf_tensor("w2b", [128, 64 * 128], BF16) as w2b, \
         nc.sbuf_tensor("stg2", [128, 16 * 512], BF16) as stg2, \
         nc.sbuf_tensor("x1s", [128, 512], BF16) as x1s, \
         nc.sbuf_tensor("sq2b", [128, 512], F32) as sq2b, \
         nc.sbuf_tensor("ro1", [1, 512], F32) as ro1, \
         nc.sbuf_tensor("ro2", [1, 512], F32) as ro2, \
         nc.sbuf_tensor("oux", [128, 512], F32) as oux, \
         nc.psum_tensor("fps", [128, 512], F32) as fps, \
         nc.psum_tensor("t1p", [1, 512], F32) as t1p, \
         nc.psum_tensor("t2p", [1, 512], F32) as t2p, \
         nc.psum_tensor("ab2", [128, 512], F32) as ab2, \
         nc.psum_tensor("bb2", [128, 512], F32) as bb2:
        for tc in range(4):
            t0 = tc * 512
            for kt in range(64):
                dma_in(fhs[:, kt * 512:(kt + 1) * 512], fh_d[kt, :, t0:t0 + 512])
            for dtile in range(16):
                dma_in(w2b[:, :].rearrange("p (kt c) -> p kt c", kt=64),
                       W2pp[:, dtile * 128:(dtile + 1) * 128]
                       .rearrange("(kt p) c -> p kt c", p=128))
                for kt in range(64):
                    mm(fps[:, :], w2b[:, kt * 128:(kt + 1) * 128],
                       fhs[:, kt * 512:(kt + 1) * 512],
                       start=(kt == 0), stop=(kt == 63))
                ssl = stg2[:, dtile * 512:(dtile + 1) * 512]
                tt(ssl, fps[:, :], Y0[:, dtile * T + t0: dtile * T + t0 + 512],
                   OP.add)
                mm(t1p[:, :], cONESb[:, 0:1], ssl,
                   start=(dtile == 0), stop=(dtile == 15))
                act(sq2b[:, :], ssl, AF.Square)
                mm(t2p[:, :], cONES[:, 0:1], sq2b[:, :],
                   start=(dtile == 0), stop=(dtile == 15))
            ts(ro1[:, :], t1p[:, :], 1.0 / HID, OP.mult)
            act(ro2[:, :], ro1[:, :], AF.Square)
            ts(ro2[:, :], ro2[:, :], -1.0, OP.mult)
            stt(ro2[:, :], t2p[:, :], 1.0 / HID, ro2[:, :], OP.mult, OP.add)
            ts(ro2[:, :], ro2[:, :], 1e-5, OP.add)
            act(ro2[:, :], ro2[:, :], AF.Sqrt)
            vrecip(ro2[:, :], ro2[:, :])
            tt(ro1[:, :], ro1[:, :], ro2[:, :], OP.mult)
            ts(ro1[:, :], ro1[:, :], -1.0, OP.mult)
            mm(ab2[:, :], cONES[0:1, :], ro2[:, :])
            mm(bb2[:, :], cONES[0:1, :], ro1[:, :])
            for dtile in range(16):
                ssl = stg2[:, dtile * 512:(dtile + 1) * 512]
                tt(oux[:, :], ssl, ab2[:, :], OP.mult)
                tt(oux[:, :], oux[:, :], bb2[:, :], OP.add)
                dma_out(out_d[dtile * 128:(dtile + 1) * 128, t0:t0 + 512],
                        oux[:, :])

    ops = E.ops
    totals = dict(E.counts)
    with nc.Block() as block:
        def section(eng_name):
            def body(eng):
                cnt = {e: 0 for e in ENG}
                for (engine, fn, inc, deps) in ops:
                    if engine == eng_name:
                        for (pe_, pc_) in deps:
                            if not (pe_ == "tensor" and engine == "tensor"):
                                eng.wait_ge(sem_of[pe_], pc_)
                        instr = fn(eng)
                        instr.then_inc(sem_of[engine], inc)
                    cnt[engine] += inc
                if eng_name == "gpsimd":
                    eng.wait_ge(sem_of["gpsimd"], totals["gpsimd"])
            return body
        block.sync(section("sync"))
        block.tensor(section("tensor"))
        block.vector(section("vector"))
        block.scalar(section("scalar"))
        block.gpsimd(section("gpsimd"))

    es.close()
    return nc


def _consts():
    H = np.array([[1.0]], dtype=np.float32)
    while H.shape[0] < 128:
        H = np.block([[H, H], [H, -H]])
    H64 = H[:64, :64].copy()
    HBDm = np.zeros((128, 128), np.float32)
    HBDm[:64, :64] = H64
    HBDm[64:, 64:] = H64
    r = np.arange(128)
    SL = (r[:, None] > r[None, :]).astype(np.float32)
    UT = (r[:, None] <= r[None, :]).astype(np.float32)
    MS = (r[None, :] >= r[:, None]).astype(np.float32)
    IDN = np.eye(128, dtype=np.float32)
    E127v = np.zeros((128, 1), np.float32)
    E127v[127, 0] = 1.0
    return H, H64, HBDm, SL, UT, MS, IDN, E127v


def kernel(L0_attn_hidden_states, L0_attn_residual, L0_attn_W_in, L0_attn_conv_w,
           L0_attn_conv_b, L0_attn_A_log, L0_attn_D, L0_attn_dt_bias,
           L0_attn_norm_w, L0_attn_W_out, L0_attn_mup_vector,
           L0_ffn_W1, L0_ffn_W2, L0_ffn_W3):
    bf = ml_dtypes.bfloat16
    x = np.asarray(L0_attn_hidden_states, np.float32)
    B, L, _ = x.shape
    H, H64, HBDm, SL, UT, MS, IDN, E127v = _consts()

    W2T = np.ascontiguousarray(np.asarray(L0_ffn_W2, np.float32).T)
    W2q = np.ascontiguousarray(
        W2T.reshape(64, 2, 64, HID).transpose(2, 1, 0, 3)).reshape(FFH, HID)

    shared = {
        "Win": np.ascontiguousarray(np.asarray(L0_attn_W_in, np.float32)),
        "Wout": np.ascontiguousarray(np.asarray(L0_attn_W_out, np.float32)),
        "W1T": np.ascontiguousarray(np.asarray(L0_ffn_W1, np.float32).T),
        "W3T": np.ascontiguousarray(np.asarray(L0_ffn_W3, np.float32).T),
        "W2q": W2q,
        "convw": np.ascontiguousarray(np.asarray(L0_attn_conv_w, np.float32)),
        "convb": np.asarray(L0_attn_conv_b, np.float32).reshape(CONV, 1),
        "mupc": np.concatenate([np.asarray(L0_attn_mup_vector, np.float32),
            np.zeros(67 * 128 - PROJ, np.float32)]).reshape(67 * 128, 1),
        "normw": np.asarray(L0_attn_norm_w, np.float32).reshape(DS, 1),
        "Dexp": np.repeat(np.asarray(L0_attn_D, np.float32), P).reshape(DS, 1),
        "Acol": np.asarray(L0_attn_A_log, np.float32).reshape(NH, 1),
        "Arow": np.asarray(L0_attn_A_log, np.float32).reshape(1, NH),
        "dtbc": np.asarray(L0_attn_dt_bias, np.float32).reshape(NH, 1),
        "ONES": np.ones((128, 128), np.float32),
        "ONESb": np.ones((128, 128), bf),
        "E127": E127v,
        "SLOW": SL, "UTRI": UT,
        "MSK01": MS.astype(bf),
        "IDNb": IDN.astype(bf), "IDNf": IDN,
        "H128": H.astype(bf), "HBD": HBDm.astype(bf),
    }

    in_maps = []
    for c in range(8):
        b, h = c // 2, c % 2
        xT = np.ascontiguousarray(x[b].T)
        own = xT[:, h * T:(h + 1) * T]
        ext = np.zeros((HID, TEXT), np.float32)
        ext[:, 128:] = own
        if h == 1:
            ext[:, 125:128] = xT[:, T - 3:T]
        m = dict(shared)
        m["x_fm"] = ext
        if h == 1:
            pext = np.zeros((HID, TEXT), np.float32)
            pext[:, 128:] = xT[:, 0:T]
            m["x_pfm"] = pext
        else:
            m["x_pfm"] = np.zeros((HID, TEXT), np.float32)
        m["maskc"] = np.full((128, 1), float(h), np.float32)
        in_maps.append(m)

    nc = build_nc()
    res = run_bass_kernel_spmd(nc, in_maps, core_ids=list(range(8)))
    out = np.empty((B, L, HID), np.float32)
    for c in range(8):
        b, h = c // 2, c % 2
        out[b, h * T:(h + 1) * T, :] = res.results[c]["out_d"].T
    return out
